# version 10
# baseline (speedup 1.0000x reference)
import contextlib
from collections import defaultdict

import numpy as np
import ml_dtypes
import concourse.bass as bass
import concourse.mybir as mybir
from concourse.bass_utils import run_bass_kernel_spmd

F32 = mybir.dt.float32
BF16 = mybir.dt.bfloat16
I32 = mybir.dt.int32
AF = mybir.ActivationFunctionType
ALU = mybir.AluOpType
AX = mybir.AxisListType

BFNP = np.dtype(ml_dtypes.bfloat16)

B, S, D = 2, 4096, 2048
NCORES = 8
TPC = 1024
NT = TPC // 128
DC = D // 128
EPS = 1e-5

ENGINES = ("tensor", "vector", "scalar", "gpsimd", "sync")
ESEM = {"tensor": "pe", "vector": "dv", "scalar": "ac", "gpsimd": "pl"}
QENG = {"sp": "sync", "act": "scalar", "sw": "gpsimd"}

DBG_ATTN = False
GROUPS_B = [[0, 1, 2, 3], [4, 5, 6, 7]]
GROUP_ALL = [[0, 1, 2, 3, 4, 5, 6, 7]]


class Em:
    def __init__(self, nc, es):
        self.nc = nc
        self.es = es
        self.blocks = []
        self.cur = {e: [] for e in ENGINES}
        self.cur_has = False
        self.sems = {}
        self.cnt = defaultdict(int)
        self.hw = {e: defaultdict(int) for e in ENGINES}
        self.writes = defaultdict(list)
        self.reads = defaultdict(list)
        self.KRING = 24
        self.qring = {q: [f"d{q}_{i}" for i in range(self.KRING)]
                      for q in ("sp", "act", "sw")}
        self.qidx = {q: 0 for q in ("sp", "act", "sw")}
        self.dma_sems = {s for ring in self.qring.values() for s in ring} | {"cc"}

    def sem(self, name):
        if name not in self.sems:
            self.sems[name] = self.es.enter_context(self.nc.semaphore(name))
        return self.sems[name]

    def rotate_qsems(self, tag):
        pass

    def new_block(self):
        if not self.cur_has:
            return
        self.blocks.append(self.cur)
        self.cur = {e: [] for e in ENGINES}
        self.cur_has = False
        for w in (self.writes, self.reads):
            for name in list(w):
                w[name] = [r for r in w[name] if r[1] in self.dma_sems]
                if not w[name]:
                    del w[name]
        allsems = set()
        for e in ENGINES:
            allsems |= set(self.hw[e])
        for s in allsems:
            m = max(self.hw[e][s] for e in ENGINES)
            for e in ENGINES:
                self.hw[e][s] = m

    @staticmethod
    def _cf(a, b):
        return a is None or b is None or a == b

    def _deps(self, rd, wr):
        need = {}

        def add(sem, val):
            if need.get(sem, 0) < val:
                need[sem] = val

        for t, slot in rd:
            for slot2, sem, val in self.writes.get(t, ()):
                if self._cf(slot, slot2):
                    add(sem, val)
        for t, slot in wr:
            for slot2, sem, val in self.writes.get(t, ()):
                if self._cf(slot, slot2):
                    add(sem, val)
            for slot2, sem, val in self.reads.get(t, ()):
                if self._cf(slot, slot2):
                    add(sem, val)
        return need

    def _filter(self, eng, need):
        hw = self.hw[eng]
        waits = [(s, v) for s, v in need.items() if hw[s] < v]
        for s, v in waits:
            hw[s] = v
        return [(self.sem(s), v) for s, v in waits]

    def _record(self, rd, wr, sem, val):
        for t, slot in wr:
            lst = self.writes[t]
            if slot is not None:
                lst[:] = [r for r in lst if r[0] != slot]
            else:
                lst.clear()
            lst.append((slot, sem, val))
            rl = self.reads.get(t)
            if rl is not None:
                if slot is None:
                    rl.clear()
                else:
                    rl[:] = [r for r in rl if r[0] != slot]
        for t, slot in rd:
            lst = self.reads[t]
            if len(lst) > 128:
                old, keep = lst[:64], lst[64:]
                agg = {}
                for s2, sm, v in old:
                    if agg.get(sm, 0) < v:
                        agg[sm] = v
                lst[:] = [(None, sm, v) for sm, v in agg.items()] + keep
            lst.append((slot, sem, val))

    def op(self, eng, fn, rd=(), wr=()):
        need = self._deps(rd, wr)
        sem_name = ESEM[eng]
        self.cnt[sem_name] += 1
        val = self.cnt[sem_name]
        wait_hs = self._filter(eng, need)
        sem_h = self.sem(sem_name)

        def run(e, fn=fn, wait_hs=wait_hs, sem_h=sem_h):
            for sh, v in wait_hs:
                e.wait_ge(sh, v)
            fn(e).then_inc(sem_h, 1)

        self.cur[eng].append(run)
        self.cur_has = True
        self._record(rd, wr, sem_name, val)
        return val

    @staticmethod
    def _dram_name(ap):
        t = getattr(ap, "tensor", None)
        if t is not None and type(t).__name__ == "DRamTensorHandle":
            return t.name
        return None

    def dma(self, q, out, in_, rd=(), wr=(), **kw):
        eng = QENG[q]
        sem_name = self.qring[q][self.qidx[q] % self.KRING]
        self.qidx[q] += 1
        rd = list(rd)
        wr = list(wr)
        declared = {t for t, _ in rd} | {t for t, _ in wr}
        nm = self._dram_name(in_)
        if nm is not None and nm not in declared:
            rd.append((nm, None))
        nm = self._dram_name(out)
        if nm is not None and nm not in declared:
            wr.append((nm, None))
        need = self._deps(rd, wr)
        prev = self.cnt[sem_name]
        if prev > 0 and need.get(sem_name, 0) < prev:
            need[sem_name] = prev
        self.cnt[sem_name] += 16
        val = self.cnt[sem_name]
        wait_hs = self._filter(eng, need)
        sem_h = self.sem(sem_name)

        def run(e, out=out, in_=in_, wait_hs=wait_hs, sem_h=sem_h, kw=kw):
            for sh, v in wait_hs:
                e.wait_ge(sh, v)
            e.dma_start(out=out, in_=in_, **kw).then_inc(sem_h, 16)

        self.cur[eng].append(run)
        self.cur_has = True
        self._record(rd, wr, sem_name, val)
        return sem_name, val

    def dma_custom(self, q, fn, rd=(), wr=()):
        eng = QENG[q]
        sem_name = self.qring[q][self.qidx[q] % self.KRING]
        self.qidx[q] += 1
        need = self._deps(rd, wr)
        prev = self.cnt[sem_name]
        if prev > 0 and need.get(sem_name, 0) < prev:
            need[sem_name] = prev
        self.cnt[sem_name] += 16
        val = self.cnt[sem_name]
        wait_hs = self._filter(eng, need)
        sem_h = self.sem(sem_name)

        def run(e, fn=fn, wait_hs=wait_hs, sem_h=sem_h):
            for sh, v in wait_hs:
                e.wait_ge(sh, v)
            ins = fn(e)
            ins.then_inc(sem_h, 16)

        self.cur[eng].append(run)
        self.cur_has = True
        self._record(rd, wr, sem_name, val)
        return sem_name, val

    def collective(self, kind, op, groups, ins_ap, outs_ap, rd=(), wr=()):
        need = self._deps(rd, wr)
        self.cnt["cc"] += 1
        val = self.cnt["cc"]
        wait_hs = self._filter("gpsimd", need)
        sem_h = self.sem("cc")

        def run(e, wait_hs=wait_hs, sem_h=sem_h):
            for sh, v in wait_hs:
                e.wait_ge(sh, v)
            e.collective_compute(kind, op, replica_groups=groups,
                                 ins=ins_ap, outs=outs_ap).then_inc(sem_h)

        self.cur["gpsimd"].append(run)
        self.cur_has = True
        self._record(rd, wr, "cc", val)
        return val

    def raw_wait(self, eng, sem_name, val):
        if self.hw[eng][sem_name] >= val:
            return
        self.hw[eng][sem_name] = val
        sh = self.sem(sem_name)
        self.cur[eng].append(lambda e, sh=sh, v=val: e.wait_ge(sh, v))
        self.cur_has = True

    def fence_all(self, eng="gpsimd"):
        for s in sorted(self.dma_sems):
            if self.cnt[s] > 0:
                self.raw_wait(eng, s, self.cnt[s])

    def realize(self):
        self.new_block()
        nc = self.nc
        for blk_ops in self.blocks:
            with nc.Block() as blk:
                for ename in ENGINES:
                    lst = blk_ops[ename]
                    if lst:
                        getattr(blk, ename)(lambda e, lst=lst: [f(e) for f in lst])


def rope_tables_interleaved(rot_dim, positions):
    inv = 1.0 / (10000.0 ** (np.arange(0, rot_dim, 2, dtype=np.float64) / rot_dim))
    fr = positions[:, None].astype(np.float64) * inv[None, :]
    cos = np.repeat(np.cos(fr), 2, axis=1)
    sin = np.repeat(np.sin(fr), 2, axis=1)
    return cos.astype(np.float32), sin.astype(np.float32)


def rope_tables_half(d_head, positions):
    inv = 1.0 / (10000.0 ** (np.arange(0, d_head, 2, dtype=np.float64) / d_head))
    fr = positions[:, None].astype(np.float64) * inv[None, :]
    emb = np.concatenate([fr, fr], axis=1)
    return np.cos(emb).astype(np.float32), np.sin(emb).astype(np.float32)


def perm_interleaved(n):
    P = np.zeros((n, n), np.float32)
    for i in range(n // 2):
        P[2 * i, 2 * i + 1] = -1.0
        P[2 * i + 1, 2 * i] = 1.0
    return P


def perm_half(n):
    P = np.zeros((n, n), np.float32)
    h = n // 2
    for i in range(h):
        P[i, i + h] = -1.0
        P[i + h, i] = 1.0
    return P


def host_prepare(inputs):
    d = {}
    w_qkv = inputs["L0_attn_a_w_qkv"]
    local = 512
    rq, rv, rk = [], [], []
    for m in range(4):
        base = m * 3 * local
        rq.append(np.arange(base, base + local))
        rv.append(np.arange(base + local, base + 2 * local))
        rk.append(np.arange(base + 2 * local, base + 3 * local))
    d["l0a_wq"] = np.ascontiguousarray(w_qkv[np.concatenate(rq)].T)
    d["l0a_wk"] = np.ascontiguousarray(w_qkv[np.concatenate(rk)].T)
    d["l0a_wv"] = np.ascontiguousarray(w_qkv[np.concatenate(rv)].T)
    d["l0a_wo"] = np.ascontiguousarray(inputs["L0_attn_a_w_out"].T)
    d["l0b_wq"] = np.ascontiguousarray(inputs["L0_attn_b_Wq"])
    d["l0b_wk"] = np.ascontiguousarray(inputs["L0_attn_b_Wk"])
    d["l0b_wv"] = np.ascontiguousarray(inputs["L0_attn_b_Wv"])
    d["l0b_wo"] = np.ascontiguousarray(inputs["L0_attn_b_Wo"])
    d["l0f_gidxT"] = np.ascontiguousarray(inputs["L0_ffn_gate_idx"].T)
    d["l0f_uidxT"] = np.ascontiguousarray(inputs["L0_ffn_up_idx"].T)
    d["l0f_didxT"] = np.ascontiguousarray(inputs["L0_ffn_down_idx"].T)
    d["l0f_lut"] = np.ascontiguousarray(inputs["L0_ffn_lut"].reshape(1, 16))
    d["l1a_wq"] = np.ascontiguousarray(inputs["L1_attn_a_wq"])
    d["l1a_wk"] = np.ascontiguousarray(inputs["L1_attn_a_wk"])
    d["l1a_wv"] = np.ascontiguousarray(inputs["L1_attn_a_wv"])
    d["l1a_wo"] = np.ascontiguousarray(inputs["L1_attn_a_wo"])
    d["l1b_win"] = np.ascontiguousarray(inputs["L1_attn_b_W_in"].T)
    d["l1b_bin"] = np.ascontiguousarray(
        inputs["L1_attn_b_b_in"].reshape(48, 128).T)
    d["l1b_wout"] = np.ascontiguousarray(inputs["L1_attn_b_W_out"].T)
    d["l1b_bout"] = np.ascontiguousarray(inputs["L1_attn_b_b_out"].reshape(1, -1))
    d["l1f_wg"] = np.ascontiguousarray(inputs["L1_ffn_w_gate"])
    d["l1f_wu"] = np.ascontiguousarray(inputs["L1_ffn_w_up"])
    d["l1f_wd"] = np.ascontiguousarray(inputs["L1_ffn_w_down"])
    return d


def host_consts():
    c = {}
    c["ident"] = np.eye(128, dtype=np.float32)
    k = np.arange(128)[:, None]
    q = np.arange(128)[None, :]
    c["tri_kq"] = (k <= q).astype(np.float32)
    c["tri_ge_kq"] = (k >= q).astype(np.float32)
    kk = np.arange(128)[:, None]
    qq = np.arange(512)[None, :]
    m8 = np.zeros((128, 8 * 512), np.float32)
    for d in range(4):
        m8[:, d * 512:(d + 1) * 512] = (kk + d * 128 <= qq)
    for ei in range(4):
        m8[:, (4 + ei) * 512:(5 + ei) * 512] = (kk >= qq - ei * 128)
    c["mask5"] = m8
    pos = np.arange(S)
    cos64, sin64 = rope_tables_interleaved(64, pos)
    t = np.ones((128, S), np.float32)
    z = np.zeros((128, S), np.float32)
    t[0:64] = cos64.T
    z[0:64] = sin64.T
    c["l0a_cos"], c["l0a_sin"] = t, z
    cosh, sinh = rope_tables_half(64, pos)
    c["l1a_cos"] = np.ascontiguousarray(
        np.concatenate([cosh.T, cosh.T], axis=0)).astype(np.float32)
    c["l1a_sin"] = np.ascontiguousarray(
        np.concatenate([sinh.T, sinh.T], axis=0)).astype(np.float32)
    cosb, sinb = rope_tables_interleaved(1024, pos)
    c["l1b_cosT"] = np.ascontiguousarray(cosb.T)
    c["l1b_sinT"] = np.ascontiguousarray(sinb.T)
    P64 = np.zeros((128, 128), np.float32)
    P64[0:64, 0:64] = perm_interleaved(64)
    c["l0a_permT"] = np.ascontiguousarray(P64.T)
    Ph = np.zeros((128, 128), np.float32)
    Ph[0:64, 0:64] = perm_half(64)
    Ph[64:128, 64:128] = perm_half(64)
    c["l1a_permT"] = np.ascontiguousarray(Ph.T)
    c["l1b_permT"] = np.ascontiguousarray(perm_interleaved(128).T)
    return c


class Prog:
    def __init__(self):
        self.nc = bass.Bass("TRN2", target_bir_lowering=False)
        self.es = contextlib.ExitStack()
        self.E = Em(self.nc, self.es)
        self.dram = {}
        self.sb = {}
        self.ps = {}
        self._uniq = 0

    def dram_in(self, name, shape, dtype=F32):
        t = self.nc.dram_tensor(name, list(shape), dtype, kind="ExternalInput")
        self.dram[name] = t
        return t

    def dram_out(self, name, shape, dtype=F32):
        t = self.nc.dram_tensor(name, list(shape), dtype, kind="ExternalOutput")
        self.dram[name] = t
        return t

    def dram_tmp(self, name, shape, dtype=BF16):
        t = self.nc.dram_tensor(name, list(shape), dtype)
        self.dram[name] = t
        return t

    def sbuf(self, name, shape, dtype, es=None):
        self._uniq += 1
        t = (es or self.es).enter_context(
            self.nc.sbuf_tensor(f"{name}_{self._uniq}", list(shape), dtype))
        self.sb[name] = t
        return t

    def psum(self, name, shape, dtype=F32, es=None):
        self._uniq += 1
        t = (es or self.es).enter_context(
            self.nc.psum_tensor(f"{name}_{self._uniq}", list(shape), dtype))
        self.ps[name] = t
        return t


def emit_ln_tile(P, x, it, t_out, t_out_name, t_out_off=0):
    E = P.E
    stats = P.sb["stats"]
    xs = x[:, it * D:(it + 1) * D]
    ssum = stats[:, 0:1]
    ssq = stats[:, 1:2]
    mean = stats[:, 2:3]
    va = stats[:, 3:4]
    m2 = stats[:, 4:5]
    rstd = stats[:, 5:6]
    to = t_out[:, t_out_off:t_out_off + D]

    tslot = (t_out_name, t_out_off)
    E.op("scalar", lambda e: e.activation(to, xs, AF.Copy, accum_out=ssum),
         rd=[("x", it)], wr=[("st", 0), tslot])
    E.op("scalar", lambda e: e.activation(to, xs, AF.Square, accum_out=ssq),
         rd=[("x", it)], wr=[("st", 1), tslot])
    E.op("vector", lambda e: e.tensor_scalar(mean, ssum, 1.0 / D, None, ALU.mult),
         rd=[("st", 0)], wr=[("st", 2)])
    E.op("vector", lambda e: e.tensor_scalar(va, ssq, 1.0 / D, EPS, ALU.mult, ALU.add),
         rd=[("st", 1)], wr=[("st", 3)])
    E.op("vector", lambda e: e.tensor_mul(m2, mean, mean),
         rd=[("st", 2)], wr=[("st", 4)])
    E.op("vector", lambda e: e.tensor_sub(va, va, m2),
         rd=[("st", 3), ("st", 4)], wr=[("st", 3)])
    E.op("scalar", lambda e: e.activation(va, va, AF.Sqrt),
         rd=[("st", 3)], wr=[("st", 3)])
    E.op("vector", lambda e: e.reciprocal(rstd, va),
         rd=[("st", 3)], wr=[("st", 5)])
    E.op("vector", lambda e: e.tensor_scalar(to, xs, mean, rstd,
                                             ALU.subtract, ALU.mult),
         rd=[("x", it), ("st", 2), ("st", 5)], wr=[tslot])


def emit_transpose_tile(P, src, src_name, src_slot, dst, dst_name, col_off,
                        ncols, dc=DC, src_off=0):
    E = P.E
    ident = P.sb["ident"]
    pst = P.ps["pst"]
    for cg in range(0, dc, 8):
        n = min(8, dc - cg)

        def f_tr(e, cg=cg, n=n):
            mm = None
            for j in range(n):
                c = cg + j
                mm = e.matmul(pst[:, j * 128:(j + 1) * 128],
                              src[:, src_off + c * 128:src_off + (c + 1) * 128],
                              ident[:], is_transpose=True,
                              start=(j == 0), stop=(j == n - 1))
            return mm

        E.op("tensor", f_tr, rd=[(src_name, src_slot), ("ident", None)],
             wr=[("pst", 0)])

        def f_ev(e, cg=cg, n=n):
            ins = None
            for j in range(n):
                c = cg + j
                ins = e.tensor_copy(
                    dst[:, c * ncols + col_off: c * ncols + col_off + 128],
                    pst[:, j * 128:(j + 1) * 128])
            return ins

        E.op("vector", f_ev, rd=[("pst", 0)],
             wr=[(dst_name, (col_off, cg))])


def emit_sparsify(P, h, n, out, out_name, h_dep):
    E = P.E
    habs = P.sb["habs"]
    sp4 = P.sb["sp4"]
    q = n // 4
    A = habs[:, 0:n]
    m01 = sp4[:, 0 * q:1 * q]
    n01 = sp4[:, 1 * q:2 * q]
    m23 = sp4[:, 2 * q:3 * q]
    n23 = sp4[:, 3 * q:4 * q]
    dge = sp4[:, 4 * q:5 * q]

    def av(i):
        return A[:, i::4]

    E.op("scalar", lambda e: e.activation(A, h, AF.Abs),
         rd=[h_dep], wr=[("habs", None)])
    E.op("vector", lambda e: e.tensor_tensor(m01, av(0), av(1), ALU.max),
         rd=[("habs", None)], wr=[("sp4", 0)])
    E.op("vector", lambda e: e.tensor_tensor(n01, av(0), av(1), ALU.min),
         rd=[("habs", None)], wr=[("sp4", 1)])
    E.op("vector", lambda e: e.tensor_tensor(m23, av(2), av(3), ALU.max),
         rd=[("habs", None)], wr=[("sp4", 2)])
    E.op("vector", lambda e: e.tensor_tensor(n23, av(2), av(3), ALU.min),
         rd=[("habs", None)], wr=[("sp4", 3)])
    E.op("vector", lambda e: e.tensor_tensor(dge, m01, m23, ALU.is_ge),
         rd=[("sp4", 0), ("sp4", 2)], wr=[("sp4", 4)])
    E.op("vector", lambda e: e.tensor_sub(n01, n01, n23),
         rd=[("sp4", 1), ("sp4", 3)], wr=[("sp4", 1)])
    E.op("vector", lambda e: e.tensor_mul(n01, n01, dge),
         rd=[("sp4", 1), ("sp4", 4)], wr=[("sp4", 1)])
    E.op("vector", lambda e: e.tensor_add(n01, n01, n23),
         rd=[("sp4", 1), ("sp4", 3)], wr=[("sp4", 1)])
    E.op("vector", lambda e: e.tensor_tensor(m01, m01, m23, ALU.min),
         rd=[("sp4", 0), ("sp4", 2)], wr=[("sp4", 0)])
    E.op("vector", lambda e: e.tensor_tensor(m01, m01, n01, ALU.max),
         rd=[("sp4", 0), ("sp4", 1)], wr=[("sp4", 0)])
    for i in range(4):
        E.op("vector", lambda e, i=i: e.tensor_tensor(dge, av(i), m01, ALU.is_ge),
             rd=[("habs", None), ("sp4", 0)], wr=[("sp4", 4)])
        E.op("vector", lambda e, i=i: e.tensor_mul(out[:, i::4], h[:, i::4], dge),
             rd=[h_dep, ("sp4", 4)], wr=[(out_name, i)])


def emit_sub_l1f(P, tag, wg, wu, wd):
    E = P.E
    x = P.sb["x"]
    IC = 8192 // 128
    NB = 8192 // 512
    ident = P.sb["ident"]

    for tb in range(NT // 4):
        with contextlib.ExitStack() as es_tb:
            hfm = P.sbuf("hfm", [128, IC * 512], BF16, es_tb)
            _l1f_block(P, tag, wg, wu, wd, tb, hfm)


def _l1f_block(P, tag, wg, wu, wd, tb, hfm):
    E = P.E
    x = P.sb["x"]
    IC = 8192 // 128
    NB = 8192 // 512
    ident = P.sb["ident"]
    if True:
        es = contextlib.ExitStack()
        with es:
            t_stage = P.sbuf("t_stage", [128, D], BF16, es)
            t_fm4 = P.sbuf("t_fm4", [128, DC * 512], BF16, es)
            wbuf = P.sbuf("wbuf", [128, 2 * DC * 512], BF16, es)
            h_tm = P.sbuf("h_tm", [128, 1024], F32, es)
            P.sbuf("habs", [128, 1024], F32, es)
            P.sbuf("sp4", [128, 5 * 256], F32, es)
            hsp = P.sbuf("hsp", [128, 2048], BF16, es)
            silu_g = P.sbuf("silu_g", [128, 2 * 512], BF16, es)
            psg = [P.psum("ps_a0", [128, 512], F32, es),
                   P.psum("ps_a1", [128, 512], F32, es)]
            psu = [P.psum("ps_b0", [128, 512], F32, es),
                   P.psum("ps_b1", [128, 512], F32, es)]
            P.psum("pst", [128, 1024], BF16, es)

            for itl in range(4):
                it = tb * 4 + itl
                emit_ln_tile(P, x, it, t_stage, "t_stage")
                emit_transpose_tile(P, t_stage, "t_stage", 0,
                                    t_fm4, "t_fm4", itl * 128, 512)
            E.new_block()

            ncols = 512
            pending = []

            def emit_pair_transpose(jb, itl, par):
                hnm = f"hsp{par}"
                for sub_i in range(2):
                    itl2 = itl - 1 + sub_i

                    def f_tr(e, sub_i=sub_i, par=par):
                        mm = None
                        for j in range(4):
                            mm = e.matmul(
                                P.ps["pst"][:, j * 128:(j + 1) * 128],
                                hsp[:, par * 1024 + sub_i * 512 + j * 128:
                                    par * 1024 + sub_i * 512 + (j + 1) * 128],
                                ident[:], is_transpose=True,
                                start=(j == 0), stop=(j == 3))
                        return mm

                    E.op("tensor", f_tr, rd=[(hnm, None), ("ident", None)],
                         wr=[("pst", 0)])

                    def f_ev(e, itl2=itl2, jb=jb):
                        ins = None
                        for j in range(4):
                            c = 4 * jb + j
                            ins = e.tensor_copy(
                                hfm[:, c * 512 + itl2 * 128:
                                    c * 512 + (itl2 + 1) * 128],
                                P.ps["pst"][:, j * 128:(j + 1) * 128])
                        return ins

                    E.op("vector", f_ev, rd=[("pst", 0)],
                         wr=[("hfm", (itl2, jb))])

            for jb in range(NB):
                for which, w in ((0, wg), (1, wu)):
                    for hh in range(2):
                        dst = wbuf[:, which * (DC * 512) + hh * (DC // 2) * 512:
                                   which * (DC * 512) + (hh + 1) * (DC // 2) * 512]
                        src = w[hh * 1024:(hh + 1) * 1024, jb * 512:(jb + 1) * 512]
                        E.dma("sp" if which == 0 else "act",
                              dst.rearrange("p (c n) -> p c n", n=512),
                              src.rearrange("(c p) n -> p c n", p=128),
                              rd=[], wr=[("wbuf", which)])
                for itl in range(4):
                    pp = itl % 2

                    def f_gate(e, itl=itl, pp=pp):
                        mm = None
                        for c in range(DC):
                            lhs = t_fm4[:, c * ncols + itl * 128:
                                        c * ncols + (itl + 1) * 128]
                            mm = e.matmul(psg[pp][:], lhs,
                                          wbuf[:, c * 512:(c + 1) * 512],
                                          start=(c == 0), stop=(c == DC - 1))
                        return mm

                    def f_up(e, itl=itl, pp=pp):
                        mm = None
                        for c in range(DC):
                            lhs = t_fm4[:, c * ncols + itl * 128:
                                        c * ncols + (itl + 1) * 128]
                            mm = e.matmul(psu[pp][:], lhs,
                                          wbuf[:, DC * 512 + c * 512:
                                               DC * 512 + (c + 1) * 512],
                                          start=(c == 0), stop=(c == DC - 1))
                        return mm

                    E.op("tensor", f_up,
                         rd=[("t_fm4", None), ("wbuf", 1)], wr=[("ps_b", pp)])
                    E.op("tensor", f_gate,
                         rd=[("t_fm4", None), ("wbuf", 0)], wr=[("ps_a", pp)])
                    E.op("scalar", lambda e, pp=pp: e.activation(
                        silu_g[:, pp * 512:(pp + 1) * 512], psg[pp][:], AF.Silu),
                        rd=[("ps_a", pp)], wr=[("silu", pp)])
                    E.op("vector", lambda e, itl=itl, pp=pp: e.tensor_mul(
                        h_tm[:, pp * 512:(pp + 1) * 512],
                        silu_g[:, pp * 512:(pp + 1) * 512], psu[pp][:]),
                        rd=[("silu", pp), ("ps_b", pp)], wr=[("h_tm", pp)])
                    if itl % 2 == 0:
                        continue
                    par = (jb * 2 + itl // 2) % 2
                    emit_sparsify(P, h_tm[:, 0:1024], 1024,
                                  hsp[:, par * 1024:par * 1024 + 1024],
                                  f"hsp{par}", ("h_tm", None))
                    for fn in pending:
                        fn()
                    pending = [lambda jb=jb, itl=itl, par=par:
                               emit_pair_transpose(jb, itl, par)]
            for fn in pending:
                fn()
            E.new_block()
        es2 = contextlib.ExitStack()
        with es2:
            wdbuf = P.sbuf("wdbuf", [128, 2 * 1024], BF16, es2)
            for tt in range(4):
                for db in range(2):
                    P.psum(f"ps_dn{tt}_{db}", [128, 512], F32, es2)
            hfm = P.sb["hfm"]
            emit_ffn_down(P, x, hfm, wd, tb * 4, IC)
            E.new_block()
            E.fence_all()
            E.new_block()


def emit_ffn_down(P, x, hfm, wd, it0, IC):
    E = P.E
    wdbuf = P.sb["wdbuf"]
    for dhalf in range(2):
        for ic in range(IC):
            wslot = ic % 2
            E.dma("act" if wslot == 0 else "sp",
                  wdbuf[:, wslot * 1024:(wslot + 1) * 1024],
                  wd[ic * 128:(ic + 1) * 128, dhalf * 1024:(dhalf + 1) * 1024],
                  rd=[], wr=[("wdbuf", wslot)])
            for tt in range(4):
                def f_mm(e, ic=ic, tt=tt, wslot=wslot):
                    mm = None
                    lhs = hfm[:, ic * 512 + tt * 128: ic * 512 + (tt + 1) * 128]
                    for db in range(2):
                        mm = e.matmul(
                            P.ps[f"ps_dn{tt}_{db}"][:], lhs,
                            wdbuf[:, wslot * 1024 + db * 512:
                                  wslot * 1024 + (db + 1) * 512],
                            start=(ic == 0), stop=(ic == IC - 1),
                            skip_group_check=True)
                    return mm

                E.op("tensor", f_mm,
                     rd=[("hfm", None), ("wdbuf", wslot)],
                     wr=[("ps_dn", (tt, 0)), ("ps_dn", (tt, 1))])
        for tt in range(4):
            it = it0 + tt
            for db in range(2):
                def f_ev(e, it=it, tt=tt, db=db, dhalf=dhalf):
                    xs = x[:, it * D + dhalf * 1024 + db * 512:
                           it * D + dhalf * 1024 + (db + 1) * 512]
                    return e.tensor_add(xs, xs, P.ps[f"ps_dn{tt}_{db}"][:])

                E.op("vector", f_ev,
                     rd=[("ps_dn", (tt, db)), ("x", it)], wr=[("x", it)])


def emit_attn_prologue(P):
    E = P.E
    x = P.sb["x"]
    with contextlib.ExitStack() as es:
        t_stage = P.sbuf("t_stage", [128, D], BF16, es)
        t_fml = P.sbuf("t_fml", [128, DC * 1024], BF16, es)
        P.psum("pst", [128, 1024], BF16, es)
        for it in range(NT):
            emit_ln_tile(P, x, it, t_stage, "t_stage")
            emit_transpose_tile(P, t_stage, "t_stage", 0,
                                t_fml, "t_fml", it * 128, 1024)
        for cg in range(0, DC, 4):
            E.dma("sp",
                  P.dram["t_ag_in"][cg * 128:(cg + 4) * 128, :]
                  .rearrange("(c p) n -> p c n", p=128),
                  t_fml[:, cg * 1024:(cg + 4) * 1024]
                  .rearrange("p (c n) -> p c n", n=1024),
                  rd=[("t_fml", None)], wr=[("t_ag_in", cg)])
        E.new_block()
    E.collective("AllGather", ALU.bypass, GROUPS_B,
                 [P.dram["t_ag_in"].ap().opt()], [P.dram["t_ag"].ap().opt()],
                 rd=[("t_ag_in", None)], wr=[("t_ag", None)])
    E.new_block()


ATTN_CFG = {
    "l0a": dict(qh=16, kvh=16, dh=128, window=None, rope="i64",
                wq="l0a_wq", wk="l0a_wk", wv="l0a_wv", wo="l0a_wo",
                cos="l0a_cos", sin="l0a_sin", perm="l0a_permT"),
    "l0b": dict(qh=16, kvh=4, dh=128, window=1024, rope=None,
                wq="l0b_wq", wk="l0b_wk", wv="l0b_wv", wo="l0b_wo"),
    "l1a": dict(qh=32, kvh=8, dh=64, window=512, rope="h64",
                wq="l1a_wq", wk="l1a_wk", wv="l1a_wv", wo="l1a_wo",
                cos="l1a_cos", sin="l1a_sin", perm="l1a_permT"),
}


def emit_sub_attn(P, tag, post_hook=None):
    cfg = ATTN_CFG[tag]
    E = P.E
    x = P.sb["x"]
    emit_attn_prologue(P)

    qh_pc = cfg["qh"] // 4
    kv_pc = cfg["kvh"] // 4
    gqa = cfg["qh"] // cfg["kvh"]
    dh = cfg["dh"]
    W = cfg["window"]
    scale = 1.0 / float(np.sqrt(dh))
    NQT = S // 128
    NQB = S // 512
    t_ag = P.dram["t_ag"]
    qcols = qh_pc * dh
    kcols = kv_pc * dh
    has_rope = cfg.get("rope") is not None
    hpg = 2 if gqa == 1 else qh_pc
    ngrp = qh_pc // hpg
    kv_pg = hpg // gqa if gqa > 1 else hpg
    nqb_buf = hpg if dh == 128 else hpg // 2
    nkb_buf = kv_pg if dh == 128 else max(1, kv_pg // 2)

    def qloc(hl):
        if dh == 128:
            return hl, 0
        half = hpg // 2
        return hl % half, 64 * (hl // half)

    def kloc(kl):
        if dh == 128:
            return kl, 0
        half = max(1, kv_pg // 2)
        return kl % half, 64 * (kl // half)

    with contextlib.ExitStack() as es:
        wqh = P.sbuf("wqh", [128, DC * qcols], BF16, es)
        wkh = P.sbuf("wkh", [128, DC * kcols], BF16, es)
        wvh = P.sbuf("wvh", [128, DC * kcols], BF16, es)
        qb_a = P.sbuf("qb_a", [128, nqb_buf * S], BF16, es)
        kb_a = P.sbuf("kb_a", [128, nkb_buf * S], BF16, es)
        vb_a = P.sbuf("vb_a", [128, kv_pg * NQT * dh], BF16, es)
        nslab = 1
        tslab = P.sbuf("tslab", [128, nslab * DC * 512], BF16, es)
        qtmp = P.sbuf("qtmp", [128, 512], F32, es)
        rtmp = P.sbuf("rtmp", [128, 512], F32, es)
        RB = 4
        att = P.sbuf("att", [128, RB * 512], BF16, es)
        ev = P.sbuf("ev", [128, 512], BF16, es)
        evo = P.sbuf("evo", [128, 512], BF16, es)
        negm = P.sbuf("negm", [1, S], BF16, es)
        maxb = P.sbuf("maxb", [128, 16], F32, es)
        sums = P.sbuf("sums", [1, 512], F32, es)
        onesrow_b = P.sbuf("onesrow_b", [1, 128], BF16, es)
        onescol_b = P.sbuf("onescol_b", [128, 1], BF16, es)
        onesrow_f = P.sbuf("onesrow_f", [1, 128], F32, es)
        identf = P.sbuf("identf", [128, 128], F32, es)
        nmask = 4 if W is None else 8
        trm5 = P.sbuf("trm5", [128, nmask * 512], BF16, es)
        if has_rope:
            rope_c = P.sbuf("rope_c", [128, 512], F32, es)
            rope_s = P.sbuf("rope_s", [128, 512], F32, es)
            permb = P.sbuf("permb", [128, 128], BF16, es)

        psA = [P.psum("psA0", [128, 512], F32, es),
               P.psum("psA1", [128, 512], F32, es)]
        psB = [P.psum("psB0", [128, 512], F32, es),
               P.psum("psB1", [128, 512], F32, es),
               P.psum("psB2", [128, 512], F32, es)]
        psB5 = psB + psA
        BNM = ["psB0n", "psB1n", "psB2n", "psA0n", "psA1n"]
        psAV = P.psum("psAV", [128, 512], F32, es)
        psSm = P.psum("psSm", [1, 512], F32, es)
        pstv = P.psum("pstv", [128, 512], BF16, es)

        E.op("gpsimd", lambda e: e.memset(onesrow_b[:], 1.0), wr=[("o1", 0)])
        E.op("gpsimd", lambda e: e.memset(onescol_b[:], 1.0), wr=[("o2", 0)])
        E.op("gpsimd", lambda e: e.memset(onesrow_f[:], 1.0), wr=[("o3", 0)])
        E.dma("sp", identf[:], P.dram["ident_in"][:, :], wr=[("identf", None)])
        E.dma("sw", trm5[:], P.dram["mask5_in"][:, 0:nmask * 512],
              wr=[("trm5", None)])
        if has_rope:
            E.dma("sw", permb[:], P.dram[cfg["perm"] + "_in"][:, :],
                  wr=[("permb", None)])
        for nm, buf, cols in ((tag + "_wq_sl", wqh, qcols),
                              (tag + "_wk_sl", wkh, kcols),
                              (tag + "_wv_sl", wvh, kcols)):
            E.dma("sw", buf[:, 0:DC * cols].rearrange("p (c n) -> p c n", n=cols),
                  P.dram[nm][:, :].rearrange("(c p) n -> p c n", p=128),
                  wr=[(nm, None)])
        E.new_block()

        def load_slab(tb8):
            sl = tb8 % nslab
            r = tb8 // 2
            col0 = (tb8 % 2) * 512
            for dc in range(DC):
                E.dma("sp", tslab[:, sl * (DC * 512) + dc * 512:
                                  sl * (DC * 512) + (dc + 1) * 512],
                      t_ag[r * D + dc * 128:r * D + (dc + 1) * 128,
                           col0:col0 + 512],
                      rd=[("t_ag", None)], wr=[("tslab", (sl, dc))])
            if has_rope:
                E.dma("sp", rope_c[:],
                      P.dram[cfg["cos"] + "_in"][:, tb8 * 512:(tb8 + 1) * 512],
                      wr=[("rope_c", 0)])
                E.dma("sp", rope_s[:],
                      P.dram[cfg["sin"] + "_in"][:, tb8 * 512:(tb8 + 1) * 512],
                      wr=[("rope_s", 0)])

        def rope_mm(psum_i, tb8, dst, dst_name, dst_slot, b0, nh):
            rsl = 0
            E.op("vector", lambda e: e.tensor_copy(ev[0:nh, :],
                                                   psA[psum_i][0:nh, :]),
                 rd=[(BNM[3 + psum_i], None)], wr=[("ev", None)])

            def f_rot(e):
                return e.matmul(psB[0][0:nh, :], permb[0:nh, 0:nh], ev[0:nh, :],
                                start=True, stop=True)

            E.op("tensor", f_rot, rd=[("permb", None), ("ev", None)],
                 wr=[(BNM[0], None)])
            E.op("vector", lambda e: e.tensor_mul(
                qtmp[0:nh, :], psA[psum_i][0:nh, :],
                rope_c[0:nh, 0:512]),
                rd=[(BNM[3 + psum_i], None), ("rope_c", rsl)], wr=[("qtmp", None)])
            E.op("vector", lambda e: e.tensor_mul(
                rtmp[0:nh, :], psB[0][0:nh, :],
                rope_s[0:nh, 0:512]),
                rd=[(BNM[0], None), ("rope_s", rsl)], wr=[("rtmp", None)])
            E.op("vector", lambda e: e.tensor_add(
                dst[b0:b0 + nh, tb8 * 512:(tb8 + 1) * 512],
                qtmp[0:nh, :], rtmp[0:nh, :]),
                rd=[("qtmp", None), ("rtmp", None)], wr=[(dst_name, dst_slot)])

        if dh == 64:
            qcur = P.sbuf("qcur", [128, S], BF16, es)
            kcur = P.sbuf("kcur", [128, S], BF16, es)

        for grp in range(ngrp):
            h0 = grp * hpg
            kv0 = h0 // gqa
            for tb8 in range(NQB):
                sl = tb8 % nslab
                load_slab(tb8)

                def projmm(e, wbuf_, cols, hcol, nh, pi, sl=sl):
                    mm = None
                    for dc in range(DC):
                        lhs = wbuf_[:, dc * cols + hcol: dc * cols + hcol + nh]
                        mm = e.matmul(psA[pi][0:nh, :], lhs,
                                      tslab[:, sl * (DC * 512) + dc * 512:
                                            sl * (DC * 512) + (dc + 1) * 512],
                                      start=(dc == 0), stop=(dc == DC - 1),
                                      skip_group_check=True)
                    return mm

                pi = 0
                for hl in range(h0, h0 + hpg):
                    bq, rq = qloc(hl - h0)
                    E.op("tensor",
                         lambda e, hl=hl, pi=pi, sl=sl:
                         projmm(e, wqh, qcols, hl * dh, dh, pi, sl),
                         rd=[(tag + "_wq_sl", None), ("tslab", (sl, None))]
                         if False else
                         [(tag + "_wq_sl", None)] + [("tslab", (sl, dc)) for dc in range(DC)],
                         wr=[(BNM[3 + pi], None)])
                    if has_rope:
                        rope_mm(pi, tb8, qb_a[:, bq * S:(bq + 1) * S], "qb_a",
                                (bq, tb8), rq, dh)
                    else:
                        E.op("vector", lambda e, bq=bq, rq=rq, tb8=tb8, pi=pi:
                             e.tensor_copy(
                                 qb_a[rq:rq + dh,
                                      bq * S + tb8 * 512:bq * S + (tb8 + 1) * 512],
                                 psA[pi][0:dh, :]),
                             rd=[(BNM[3 + pi], None)], wr=[("qb_a", (bq, tb8))])
                    pi ^= 1
                for kl in range(kv0, kv0 + kv_pg):
                    bk, rk = kloc(kl - kv0)
                    E.op("tensor",
                         lambda e, kl=kl, pi=pi, sl=sl:
                         projmm(e, wkh, kcols, kl * dh, dh, pi, sl),
                         rd=[(tag + "_wk_sl", None)] +
                         [("tslab", (sl, dc)) for dc in range(DC)],
                         wr=[(BNM[3 + pi], None)])
                    if has_rope:
                        rope_mm(pi, tb8, kb_a[:, bk * S:(bk + 1) * S], "kb_a",
                                (bk, tb8), rk, dh)
                    else:
                        E.op("vector", lambda e, bk=bk, rk=rk, tb8=tb8, pi=pi:
                             e.tensor_copy(
                                 kb_a[rk:rk + dh,
                                      bk * S + tb8 * 512:bk * S + (tb8 + 1) * 512],
                                 psA[pi][0:dh, :]),
                             rd=[(BNM[3 + pi], None)], wr=[("kb_a", (bk, tb8))])
                    pi ^= 1
                    E.op("tensor",
                         lambda e, kl=kl, pi=pi, sl=sl:
                         projmm(e, wvh, kcols, kl * dh, dh, pi, sl),
                         rd=[(tag + "_wv_sl", None)] +
                         [("tslab", (sl, dc)) for dc in range(DC)],
                         wr=[(BNM[3 + pi], None)])
                    E.op("vector", lambda e, pi=pi: e.tensor_copy(
                        ev[0:dh, :], psA[pi][0:dh, :]),
                        rd=[(BNM[3 + pi], None)], wr=[("ev", None)])

                    def f_tr(e):
                        mm = None
                        for j in range(4):
                            mm = e.matmul(
                                pstv[0:128, j * dh:(j + 1) * dh],
                                ev[0:dh, j * 128:(j + 1) * 128],
                                P.sb["ident"][0:dh, 0:dh], is_transpose=True,
                                start=(j == 0), stop=(j == 3))
                        return mm

                    E.op("tensor", f_tr, rd=[("ev", None), ("ident", None)],
                         wr=[("pstv", None)])
                    klg = kl - kv0
                    E.op("vector", lambda e, klg=klg, tb8=tb8: e.tensor_copy(
                        vb_a[:, (klg * NQT + tb8 * 4) * dh:
                             (klg * NQT + tb8 * 4 + 4) * dh],
                        pstv[0:128, 0:4 * dh]),
                        rd=[("pstv", None)], wr=[("vb_a", (klg, tb8))])
                    pi ^= 1
            E.new_block()

            last_kcopy = [None]
            for hl in range(h0, h0 + hpg):
                bq, rq = qloc(hl - h0)
                klg = (hl // gqa) - kv0
                bk, rk = kloc(klg)
                qview = qb_a[rq:rq + dh, bq * S:(bq + 1) * S]
                kview = kb_a[rk:rk + dh, bk * S:(bk + 1) * S]
                if dh == 64 and rq != 0:
                    E.dma("sp", qcur[0:dh, :], qview,
                          rd=[("qb_a", None)], wr=[("qcur", None)])
                    qview = qcur[0:dh, :]
                if dh == 64 and rk != 0:
                    if last_kcopy[0] != (bk, rk):
                        E.dma("sp", kcur[0:dh, :], kview,
                              rd=[("kb_a", None)], wr=[("kcur", None)])
                        last_kcopy[0] = (bk, rk)
                    kview = kcur[0:dh, :]

                for qt in range(NQT):
                    q0 = qt * 128
                    kb_hi = (q0 + 127) // 512
                    kb_lo = 0 if W is None else max(0, (q0 - W)) // 512
                    nkb = kb_hi - kb_lo + 1
                    for i, kb in enumerate(range(kb_lo, kb_hi + 1)):
                        pi = i % 2

                        def f_mm(e, qt=qt, kb=kb, pi=pi, qview=qview,
                                 kview=kview):
                            return e.matmul(
                                psA[pi][:, :],
                                qview[:, qt * 128:(qt + 1) * 128],
                                kview[:, kb * 512:(kb + 1) * 512],
                                start=True, stop=True, skip_group_check=True)

                        E.op("tensor", f_mm,
                             rd=[("qb_a", None), ("kb_a", None),
                                 ("qcur", None), ("kcur", None)],
                             wr=[(BNM[3 + pi], None)])
                        E.op("vector", lambda e, i=i, pi=pi: e.reduce_max(
                            maxb[:, i:i + 1], psA[pi][:], axis=AX.X),
                            rd=[(BNM[3 + pi], None)], wr=[("maxb", i)])
                    E.op("vector", lambda e, nkb=nkb: e.reduce_max(
                        maxb[:, 15:16], maxb[:, 0:nkb], axis=AX.X),
                        rd=[("maxb", None)], wr=[("maxb", 15)])

                    def f_trm(e):
                        return e.matmul(psSm[:, 0:128], maxb[:, 15:16],
                                        identf[:, 0:128],
                                        is_transpose=True, start=True, stop=True)

                    E.op("tensor", f_trm, rd=[("maxb", 15), ("identf", None)],
                         wr=[("psSm", None)])
                    E.op("vector", lambda e, qt=qt: e.tensor_scalar(
                        negm[0:1, qt * 128:(qt + 1) * 128], psSm[0:1, 0:128],
                        -1.0, None, ALU.mult),
                        rd=[("psSm", None)], wr=[("negm", qt)])

                for qb in range(NQB):
                    qb0 = qb * 512
                    kt_hi = (qb0 + 511) // 128
                    kt_lo = 0 if W is None else max(0, (qb0 - W)) // 128
                    first = True
                    for idx, kt in enumerate(range(kt_lo, kt_hi + 1)):
                        par = idx % RB
                        k0 = kt * 128

                        def f_sc(e, kt=kt, qb=qb, par=par, qview=qview,
                                 kview=kview):
                            e.matmul(psB5[par][:, :],
                                     kview[:, kt * 128:(kt + 1) * 128],
                                     qview[:, qb * 512:(qb + 1) * 512],
                                     start=True, stop=False,
                                     skip_group_check=True)
                            return e.matmul(psB5[par][:, :], onesrow_b[:],
                                            negm[0:1, qb * 512:(qb + 1) * 512],
                                            start=False, stop=True,
                                            skip_group_check=True)

                        E.op("tensor", f_sc,
                             rd=[("kb_a", None), ("qb_a", None), ("negm", None),
                                 ("qcur", None), ("kcur", None), ("o1", 0)],
                             wr=[(BNM[par], None)])
                        E.op("scalar", lambda e, par=par: e.activation(
                            att[:, par * 512:(par + 1) * 512], psB5[par][:],
                            AF.Exp, scale=scale),
                            rd=[(BNM[par], None)], wr=[("att", par)])
                        db_off = k0 - qb0
                        mi = None
                        if 0 <= db_off <= 384:
                            mi = db_off // 128
                        elif W is not None and -W <= db_off <= -W + 384:
                            mi = 4 + (db_off + W) // 128
                        if mi is not None:
                            sub = att[:, par * 512:(par + 1) * 512]
                            E.op("vector", lambda e, sub=sub, mi=mi:
                                 e.tensor_mul(sub, sub,
                                              trm5[:, mi * 512:(mi + 1) * 512]),
                                 rd=[("trm5", mi), ("att", par)],
                                 wr=[("att", par)])

                        def f_av(e, kt=kt, par=par, first=first,
                                 last=(kt == kt_hi), klg=klg, rk=rk):
                            e.matmul(psAV[0:dh, :],
                                     vb_a[:, (klg * NQT + kt) * dh:
                                          (klg * NQT + kt) * dh + dh],
                                     att[:, par * 512:(par + 1) * 512],
                                     start=first, stop=last,
                                     skip_group_check=True)
                            return e.matmul(psSm[0:1, :], onescol_b[:],
                                            att[:, par * 512:(par + 1) * 512],
                                            start=first, stop=last,
                                            skip_group_check=True)
                        _ = (first, kt)

                        E.op("tensor", f_av,
                             rd=[("vb_a", None), ("att", par), ("o2", 0)],
                             wr=[("psAV", None), ("psSm", None)])
                        first = False

                    E.op("vector", lambda e: e.tensor_copy(sums[:], psSm[0:1, :]),
                         rd=[("psSm", None)], wr=[("sums", None)])
                    E.op("vector", lambda e: e.reciprocal(sums[:], sums[:]),
                         rd=[("sums", None)], wr=[("sums", None)])

                    def f_bc(e):
                        return e.matmul(psB[0][:, :], onesrow_f[:], sums[:],
                                        start=True, stop=True,
                                        skip_group_check=True)

                    E.op("tensor", f_bc, rd=[("o3", 0), ("sums", None)],
                         wr=[(BNM[0], None)])
                    E.op("vector", lambda e: e.tensor_copy(rtmp[:], psB[0][:]),
                         rd=[(BNM[0], None)], wr=[("rtmp", None)])
                    E.op("vector", lambda e: e.tensor_mul(
                        evo[0:dh, :], psAV[0:dh, :], rtmp[0:dh, :]),
                        rd=[("psAV", None), ("rtmp", None)], wr=[("evo", None)])
                    E.dma("sp",
                          P.dram["att_pub"][hl * dh:(hl + 1) * dh,
                                            qb * 512:(qb + 1) * 512],
                          evo[0:dh, :],
                          rd=[("evo", None)], wr=[("att_pub", (hl, qb))])
        E.fence_all()
        E.new_block()

    with contextlib.ExitStack() as es:
        aout = P.sbuf("aout", [128, DC * 1024], BF16, es)
        wo_sb = P.sbuf("wo_sb", [128, DC * 2048], BF16, es)
        pso0 = P.psum("pso0", [128, 512], F32, es)
        pso1 = P.psum("pso1", [128, 512], F32, es)
        pso = [pso0, pso1]
        E.dma("sw", wo_sb[:].rearrange("p (c n) -> p c n", n=2048),
              P.dram[cfg["wo"]][:, :].rearrange("(c p) n -> p c n", p=128),
              wr=[("wo_sb", None)])
        E.collective("AllGather", ALU.bypass, GROUPS_B,
                     [P.dram["att_pub"].ap().opt()],
                     [P.dram["att_all"].ap().opt()],
                     rd=[("att_pub", None)], wr=[("att_all", None)])
        if post_hook is not None:
            post_hook(P)
        rank_sb = P.sb["rank_sb"]
        for cg in range(0, DC, 4):
            P._uniq += 1

            def f_load(e, cg=cg, _u=P._uniq):
                with e.register(f"rkoff{cg}_{_u}") as r:
                    e.reg_load(r, rank_sb[0:1, 0:1])
                    e.reg_mul(r, r, 1024)
                    e.reg_add(r, r, cg * 128 * S)
                    src_ap = bass.AP(P.dram["att_all"], r,
                                     [[S, 128], [128 * S, 4], [1, 1024]])
                    return e.dma_start(
                        out=aout[:, cg * 1024:(cg + 4) * 1024]
                        .rearrange("p (c n) -> p c n", n=1024),
                        in_=src_ap)

            E.dma_custom("sw", f_load,
                         rd=[("att_all", None), ("rank_sb", None)],
                         wr=[("aout", cg)])
        for tt in range(NT):
            for db in range(4):
                par = db % 2

                def f_mm(e, tt=tt, db=db, par=par):
                    mm = None
                    for dc in range(DC):
                        lhs = aout[:, dc * 1024 + tt * 128:
                                   dc * 1024 + (tt + 1) * 128]
                        mm = e.matmul(pso[par][:], lhs,
                                      wo_sb[:, dc * 2048 + db * 512:
                                            dc * 2048 + (db + 1) * 512],
                                      start=(dc == 0), stop=(dc == DC - 1),
                                      skip_group_check=True)
                    return mm

                E.op("tensor", f_mm,
                     rd=[("aout", None), ("wo_sb", None)], wr=[("psop", par)])

                def f_ev(e, tt=tt, db=db, par=par):
                    xs = x[:, tt * D + db * 512:tt * D + (db + 1) * 512]
                    return e.tensor_add(xs, xs, pso[par][:])

                E.op("vector", f_ev,
                     rd=[("psop", par), ("x", tt)], wr=[("x", tt)])
        E.fence_all()
        E.new_block()


def emit_sub_l1b(P, tag):
    E = P.E
    x = P.sb["x"]
    with contextlib.ExitStack() as es_outer:
        qbuf = P.sbuf("qbuf", [128, DC * 1024], BF16, es_outer)
        _emit_l1b_main(P, tag, qbuf)
        _emit_l1b_out(P, tag, qbuf)


def _emit_l1b_main(P, tag, qbuf):
    E = P.E
    x = P.sb["x"]
    with contextlib.ExitStack() as es:
        t_fml = P.sbuf("t_fml", [128, DC * 1024], BF16, es)
        with contextlib.ExitStack() as es2:
            t_stage = P.sbuf("t_stage", [128, D], BF16, es2)
            P.psum("pst", [128, 1024], BF16, es2)
            for it in range(NT):
                emit_ln_tile(P, x, it, t_stage, "t_stage")
                E.dma("sp", P.dram["t_tm_d"][it * 128:(it + 1) * 128, :],
                      t_stage[:, 0:D], rd=[("t_stage", 0)],
                      wr=[("t_tm_d", it)])
                emit_transpose_tile(P, t_stage, "t_stage", 0,
                                    t_fml, "t_fml", it * 128, 1024)
            E.fence_all()
            E.new_block()

        kvb = P.sbuf("kvb", [128, DC * 1024], BF16, es)
        kc = P.sbuf("kc", [128, 1024], F32, es)
        sqc = P.sbuf("sqc", [128, 1024], F32, es)
        wibuf = P.sbuf("wibuf", [128, 2 * DC * 128], BF16, es)
        binb = P.sbuf("binb", [128, 48], F32, es)
        ev = P.sbuf("ev", [128, 1024], BF16, es)
        qtmp = P.sbuf("qtmp", [128, 1024], F32, es)
        rtmp = P.sbuf("rtmp", [128, 1024], F32, es)
        rope_c = P.sbuf("rope_c", [128, 1024], F32, es)
        rope_s = P.sbuf("rope_s", [128, 1024], F32, es)
        permb = P.sbuf("permb", [128, 128], BF16, es)
        nrow = P.sbuf("nrow", [1, 1024], F32, es)
        wbb = P.sbuf("wbb", [128, 1024], F32, es)
        apart = P.sbuf("apart", [128, 16], F32, es)
        onescol_f = P.sbuf("onescol_f", [128, 1], F32, es)
        onesrow_f = P.sbuf("onesrow_f", [1, 128], F32, es)

        psP0 = P.psum("psP0", [128, 512], F32, es)
        psP1 = P.psum("psP1", [128, 512], F32, es)
        psS2a = P.psum("psS2a", [1, 512], F32, es)
        psS2b = P.psum("psS2b", [1, 512], F32, es)
        psS2h = [psS2a, psS2b]
        psR = P.psum("psR", [128, 512], F32, es)
        psP = [psP0, psP1]

        E.op("gpsimd", lambda e: e.memset(onescol_f[:], 1.0), wr=[("oc", 0)])
        E.op("gpsimd", lambda e: e.memset(onesrow_f[:], 1.0), wr=[("orr", 0)])
        E.dma("sw", permb[:], P.dram["l1b_perm_in"][:, :], wr=[("permb", None)])
        E.dma("sp", binb[:], P.dram["l1b_bin_in"][:, :], wr=[("binb", None)])
        E.new_block()

        def proj_chunk(c, which, bias_col):
            outc = which * DC + c
            wslot = outc % 2
            E.dma("sw", wibuf[:, wslot * (DC * 128):(wslot + 1) * (DC * 128)]
                  .rearrange("p (c n) -> p c n", n=128),
                  P.dram["l1b_win"][:, outc * 128:(outc + 1) * 128]
                  .rearrange("(c p) n -> p c n", p=128),
                  wr=[("wibuf", wslot)])
            for blk in range(2):
                def f_mm(e, wslot=wslot, blk=blk):
                    mm = None
                    for dc in range(DC):
                        lhs = wibuf[:, wslot * (DC * 128) + dc * 128:
                                    wslot * (DC * 128) + (dc + 1) * 128]
                        mm = e.matmul(psP[blk][:],
                                      lhs,
                                      t_fml[:, dc * 1024 + blk * 512:
                                            dc * 1024 + blk * 512 + 512],
                                      start=(dc == 0), stop=(dc == DC - 1),
                                      skip_group_check=True)
                    return mm

                E.op("tensor", f_mm, rd=[("wibuf", wslot), ("t_fml", None)],
                     wr=[("psP", blk)])
            for blk in range(2):
                E.op("vector", lambda e, blk=blk, bias_col=bias_col:
                     e.tensor_scalar(qtmp[:, blk * 512:(blk + 1) * 512],
                                     psP[blk][:], binb[:, bias_col:bias_col + 1],
                                     None, ALU.add),
                     rd=[("psP", blk), ("binb", None)], wr=[("qtmp", blk)])

        def rope_l1b(c, dst, dst_name, dst_f32):
            E.dma("sp", rope_c[:], P.dram["l1b_cos_sl"]
                  [c * 128:(c + 1) * 128, :], wr=[("rope_c", None)])
            E.dma("sp", rope_s[:], P.dram["l1b_sin_sl"]
                  [c * 128:(c + 1) * 128, :], wr=[("rope_s", None)])
            E.op("vector", lambda e: e.tensor_copy(ev[:], qtmp[:]),
                 rd=[("qtmp", None)], wr=[("ev", None)])
            for blk in range(2):
                def f_rot(e, blk=blk):
                    return e.matmul(psR[:], permb[:],
                                    ev[:, blk * 512:(blk + 1) * 512],
                                    start=True, stop=True)

                E.op("tensor", f_rot, rd=[("permb", None), ("ev", None)],
                     wr=[("psR", 0)])
                E.op("vector", lambda e, blk=blk: e.tensor_mul(
                    rtmp[:, blk * 512:(blk + 1) * 512], psR[:],
                    rope_s[:, blk * 512:(blk + 1) * 512]),
                    rd=[("psR", 0), ("rope_s", None)], wr=[("rtmp", blk)])
            E.op("vector", lambda e: e.tensor_mul(qtmp[:], qtmp[:], rope_c[:]),
                 rd=[("qtmp", None), ("rope_c", None)], wr=[("qtmp", None)])
            if dst_f32:
                E.op("vector", lambda e: e.tensor_add(dst, qtmp[:], rtmp[:]),
                     rd=[("qtmp", None), ("rtmp", None)], wr=[(dst_name, c)])
            else:
                E.op("vector", lambda e: e.tensor_add(dst, qtmp[:], rtmp[:]),
                     rd=[("qtmp", None), ("rtmp", None)], wr=[(dst_name, c)])

        for c in range(DC):
            proj_chunk(c, 0, c)
            if c < 8:
                rope_l1b(c, qbuf[:, c * 1024:(c + 1) * 1024], "qbuf", False)
            else:
                E.op("vector", lambda e, c=c: e.tensor_copy(
                    qbuf[:, c * 1024:(c + 1) * 1024], qtmp[:]),
                    rd=[("qtmp", None)], wr=[("qbuf", c)])
            proj_chunk(c, 1, DC + c)
            if c < 8:
                rope_l1b(c, kc[:], "kc", True)
            else:
                E.op("vector", lambda e: e.tensor_copy(kc[:], qtmp[:]),
                     rd=[("qtmp", None)], wr=[("kc", None)])
            E.op("scalar", lambda e: e.activation(sqc[:], kc[:], AF.Square),
                 rd=[("kc", None)], wr=[("sqc", None)])
            for blk in range(2):
                def f_s2(e, c=c, blk=blk):
                    return e.matmul(psS2h[blk][0:1, :],
                                    onescol_f[:], sqc[:, blk * 512:(blk + 1) * 512],
                                    start=(c == 0),
                                    stop=(c == DC - 1), skip_group_check=True)

                E.op("tensor", f_s2, rd=[("oc", 0), ("sqc", None)],
                     wr=[("psS2", blk)])
            proj_chunk(c, 2, 2 * DC + c)
            E.op("vector", lambda e, c=c: e.tensor_mul(
                kvb[:, c * 1024:(c + 1) * 1024], qtmp[:], kc[:]),
                rd=[("qtmp", None), ("kc", None)], wr=[("kvb", c)])
        E.new_block()

        for blk in range(2):
            E.op("scalar", lambda e, blk=blk: e.activation(
                nrow[0:1, blk * 512:(blk + 1) * 512], psS2h[blk][0:1, :],
                AF.Sqrt),
                rd=[("psS2", blk)], wr=[("nrow", blk)])
        E.op("vector", lambda e: e.tensor_scalar(nrow[:], nrow[:], 1e-12, None,
                                                 ALU.max),
             rd=[("nrow", None)], wr=[("nrow", None)])
        E.op("vector", lambda e: e.reciprocal(nrow[:], nrow[:]),
             rd=[("nrow", None)], wr=[("nrow", None)])
        for blk in range(2):
            def f_bc(e, blk=blk):
                return e.matmul(psR[:], onesrow_f[:],
                                nrow[0:1, blk * 512:(blk + 1) * 512],
                                start=True, stop=True)

            E.op("tensor", f_bc, rd=[("orr", 0), ("nrow", None)],
                 wr=[("psR", 0)])
            E.op("vector", lambda e, blk=blk: e.tensor_copy(
                wbb[:, blk * 512:(blk + 1) * 512], psR[:]),
                rd=[("psR", 0)], wr=[("wbb", blk)])
        for c in range(DC):
            E.op("vector", lambda e, c=c: e.tensor_mul(
                rtmp[:], kvb[:, c * 1024:(c + 1) * 1024], wbb[:]),
                rd=[("kvb", c), ("wbb", None)], wr=[("rtmp", None)])
            E.op("vector", lambda e, c=c: e.reduce_sum(
                apart[:, c:c + 1], rtmp[:], axis=AX.X),
                rd=[("rtmp", None)], wr=[("apart", c)])
        E.dma("sp", P.dram["ar_in"][:, :], apart[:],
              rd=[("apart", None)], wr=[("ar_in", None)])
        E.new_block()
        E.collective("AllReduce", ALU.add, GROUPS_B,
                     [P.dram["ar_in"].ap().opt()], [P.dram["ar_out"].ap().opt()],
                     rd=[("ar_in", None)], wr=[("ar_out", None)])
        E.dma("sp", apart[:], P.dram["ar_out"][:, :],
              rd=[("ar_out", None)], wr=[("apart", None)])
        E.new_block()
        for c in range(DC):
            E.op("vector", lambda e, c=c: e.tensor_scalar(
                qbuf[:, c * 1024:(c + 1) * 1024],
                qbuf[:, c * 1024:(c + 1) * 1024],
                apart[:, c:c + 1], None, ALU.mult),
                rd=[("qbuf", c), ("apart", None)], wr=[("qbuf", c)])
        E.fence_all()
        E.new_block()


def _emit_l1b_out(P, tag, qbuf):
    E = P.E
    x = P.sb["x"]
    with contextlib.ExitStack() as es:
        wo_sb = P.sbuf("wo_sb", [128, DC * 2048], BF16, es)
        bb = P.sbuf("bb", [128, 2048], F32, es)
        brow = P.sbuf("brow", [1, 2048], F32, es)
        onesrow_f2 = P.sbuf("onesrow_f2", [1, 128], F32, es)
        ttile = P.sbuf("ttile", [128, 2 * D], BF16, es)
        rtmp2 = P.sbuf("rtmp2", [128, 512], F32, es)
        pso0 = P.psum("pso0", [128, 512], F32, es)
        pso1 = P.psum("pso1", [128, 512], F32, es)
        psR2 = P.psum("psR2", [128, 512], F32, es)
        pso = [pso0, pso1]

        E.op("gpsimd", lambda e: e.memset(onesrow_f2[:], 1.0), wr=[("or2", 0)])
        E.dma("sp", brow[0:1, :], P.dram["l1b_bout_in"][:, :],
              wr=[("brow", None)])
        E.dma("sw", wo_sb[:].rearrange("p (c n) -> p c n", n=2048),
              P.dram["l1b_wout"][:, :].rearrange("(c p) n -> p c n", p=128),
              wr=[("wo_sb", None)])
        for db in range(4):
            def f_bc(e, db=db):
                return e.matmul(psR2[:], onesrow_f2[:],
                                brow[0:1, db * 512:(db + 1) * 512],
                                start=True, stop=True)

            E.op("tensor", f_bc, rd=[("or2", 0), ("brow", None)],
                 wr=[("psR2", 0)])
            E.op("vector", lambda e, db=db: e.tensor_copy(
                bb[:, db * 512:(db + 1) * 512], psR2[:]),
                rd=[("psR2", 0)], wr=[("bb", db)])
        E.new_block()
        for tt in range(NT):
            tslot = tt % 2
            E.dma("sp", ttile[:, tslot * D:(tslot + 1) * D],
                  P.dram["t_tm_d"][tt * 128:(tt + 1) * 128, :],
                  rd=[("t_tm_d", tt)], wr=[("ttile", tslot)])
            for db in range(4):
                par = db % 2

                def f_mm(e, tt=tt, db=db, par=par):
                    mm = None
                    for dc in range(DC):
                        lhs = qbuf[:, dc * 1024 + tt * 128:
                                   dc * 1024 + (tt + 1) * 128]
                        mm = e.matmul(pso[par][:],
                                      lhs,
                                      wo_sb[:, dc * 2048 + db * 512:
                                            dc * 2048 + (db + 1) * 512],
                                      start=(dc == 0), stop=(dc == DC - 1),
                                      skip_group_check=True)
                    return mm

                E.op("tensor", f_mm,
                     rd=[("qbuf", None), ("wo_sb", None)], wr=[("psop", par)])

                def f_e1(e, tt=tt, db=db, par=par, tslot=tslot):
                    return e.tensor_add(rtmp2[:], pso[par][:],
                                        bb[:, db * 512:(db + 1) * 512])

                E.op("vector", f_e1, rd=[("psop", par), ("bb", db)],
                     wr=[("rtmp2", None)])

                def f_e2(e, tt=tt, db=db, tslot=tslot):
                    return e.tensor_add(
                        rtmp2[:], rtmp2[:],
                        ttile[:, tslot * D + db * 512:tslot * D + (db + 1) * 512])

                E.op("vector", f_e2, rd=[("rtmp2", None), ("ttile", tslot)],
                     wr=[("rtmp2", None)])

                def f_e3(e, tt=tt, db=db):
                    xs = x[:, tt * D + db * 512:tt * D + (db + 1) * 512]
                    return e.tensor_add(xs, xs, rtmp2[:])

                E.op("vector", f_e3, rd=[("rtmp2", None), ("x", tt)],
                     wr=[("x", tt)])
        E.fence_all()
        E.new_block()


def emit_expand_l0(P):
    E = P.E
    nc = P.nc
    ident = P.sb["ident"]
    with contextlib.ExitStack() as es:
        lut_sb = P.sbuf("lut_sb", [1, 16], F32, es)
        onesr = P.sbuf("onesr", [1, 128], F32, es)
        lutb = P.sbuf("lutb", [128, 16], F32, es)
        lutI = P.sbuf("lutI", [128, 16 * 128], BF16, es)
        idxb = P.sbuf("idxb", [128, 5504], BF16, es)
        msk = P.sbuf("msk", [128, 2 * 512], BF16, es)
        wrow = P.sbuf("wrow", [128, 5504], BF16, es)
        psl = P.psum("ps_lut", [128, 16], F32, es)
        psw = P.psum("ps_w", [128, 512], F32, es)

        E.dma("sp", lut_sb[0:1, :], P.dram["lut_in"][:, :], wr=[("lut_sb", None)])
        E.op("gpsimd", lambda e: e.memset(onesr[:], 1.0), wr=[("onesr", None)])
        E.op("tensor", lambda e: e.matmul(psl[:], onesr[0:1, :], lut_sb[0:1, :],
                                          start=True, stop=True),
             rd=[("onesr", None), ("lut_sb", None)], wr=[("ps_lut", None)])
        E.op("vector", lambda e: e.tensor_copy(lutb[:], psl[:]),
             rd=[("ps_lut", None)], wr=[("lutb", None)])
        for v in range(16):
            E.op("vector", lambda e, v=v: e.tensor_scalar(
                lutI[:, v * 128:(v + 1) * 128], ident[:], lutb[:, v:v + 1],
                None, ALU.mult),
                rd=[("ident", None), ("lutb", None)], wr=[("lutI", v)])

        def expand(idx_dram, out_dram, rows, cols, rows_done_name):
            nch = (rows + 127) // 128
            for ch in range(nch):
                r0 = ch * 128
                nr = min(128, rows - r0)
                E.dma("sw", idxb[0:nr, 0:cols], idx_dram[r0:r0 + nr, :],
                      wr=[("idxb", None)])
                nblk = (cols + 511) // 512
                for ob in range(nblk):
                    c0 = ob * 512
                    nc_ = min(512, cols - c0)
                    for v in range(16):
                        E.op("vector", lambda e, v=v, nr=nr, c0=c0, nc_=nc_:
                             e.tensor_scalar(msk[0:nr, (v % 2) * 512:(v % 2) * 512 + nc_],
                                             idxb[0:nr, c0:c0 + nc_],
                                             float(v), None, ALU.is_equal),
                             rd=[("idxb", None)], wr=[("msk", v % 2)])
                        E.op("tensor", lambda e, v=v, nr=nr, nc_=nc_:
                             e.matmul(psw[0:nr, 0:nc_],
                                      lutI[0:nr, v * 128:v * 128 + nr],
                                      msk[0:nr, (v % 2) * 512:(v % 2) * 512 + nc_],
                                      start=(v == 0), stop=(v == 15)),
                             rd=[("lutI", v), ("msk", v % 2)], wr=[("ps_w", None)])
                    E.op("vector", lambda e, nr=nr, c0=c0, nc_=nc_:
                         e.tensor_copy(wrow[0:nr, c0:c0 + nc_], psw[0:nr, 0:nc_]),
                         rd=[("ps_w", None)], wr=[("wrow", ob)])
                E.dma("sp", out_dram[r0:r0 + nr, :], wrow[0:nr, 0:cols],
                      rd=[("wrow", None)], wr=[(rows_done_name, ch)])

        expand(P.dram["gidx_sl"], P.dram["wexp_g_in"], 256, 5504, "wexp_g_in")
        expand(P.dram["uidx_sl"], P.dram["wexp_u_in"], 256, 5504, "wexp_u_in")
        expand(P.dram["didx_sl"], P.dram["wexp_d_in"], 688, 2048, "wexp_d_in")
        for nm, out_nm in (("wexp_g_in", "wexp_g"), ("wexp_u_in", "wexp_u"),
                           ("wexp_d_in", "wexp_d")):
            E.collective("AllGather", ALU.bypass, GROUP_ALL,
                         [P.dram[nm].ap().opt()], [P.dram[out_nm].ap().opt()],
                         rd=[(nm, None)], wr=[(out_nm, None)])


def emit_sub_l0f(P, tag):
    E = P.E
    wg = P.dram["wexp_g"]
    wu = P.dram["wexp_u"]
    wd = P.dram["wexp_d"]
    for tb in range(NT // 4):
        with contextlib.ExitStack() as es_tb:
            hfm = P.sbuf("hfm", [128, 43 * 512], BF16, es_tb)
            _l0f_block(P, tag, wg, wu, wd, tb, hfm)


def _l0f_block(P, tag, wg, wu, wd, tb, hfm):
    E = P.E
    x = P.sb["x"]
    IC = 43
    ident = P.sb["ident"]
    es = contextlib.ExitStack()
    with es:
        t_stage = P.sbuf("t_stage", [128, D], BF16, es)
        t_fm4 = P.sbuf("t_fm4", [128, DC * 512], BF16, es)
        wbuf = P.sbuf("wbuf", [128, 2 * DC * 512], BF16, es)
        hb = P.sbuf("hb", [128, 2 * 512], BF16, es)
        silu_g = P.sbuf("silu_g", [128, 2 * 512], BF16, es)
        psg = [P.psum("ps_a0", [128, 512], F32, es),
               P.psum("ps_a1", [128, 512], F32, es)]
        psu = [P.psum("ps_b0", [128, 512], F32, es),
               P.psum("ps_b1", [128, 512], F32, es)]
        P.psum("pst", [128, 1024], BF16, es)

        for itl in range(4):
            it = tb * 4 + itl
            emit_ln_tile(P, x, it, t_stage, "t_stage")
            emit_transpose_tile(P, t_stage, "t_stage", 0,
                                t_fm4, "t_fm4", itl * 128, 512)
        E.new_block()

        ncols = 512
        NBLK = (5504 + 511) // 512
        pending = []

        def emit_tile_transpose(jb, itl, nch, pp):
            def f_tr(e, nch=nch, pp=pp):
                mm = None
                for j in range(nch):
                    mm = e.matmul(P.ps["pst"][:, j * 128:(j + 1) * 128],
                                  hb[:, pp * 512 + j * 128:
                                      pp * 512 + (j + 1) * 128],
                                  ident[:], is_transpose=True,
                                  start=(j == 0), stop=(j == nch - 1))
                return mm

            E.op("tensor", f_tr, rd=[("hb", pp), ("ident", None)],
                 wr=[("pst", 0)])

            def f_ev(e, itl=itl, jb=jb, nch=nch):
                ins = None
                for j in range(nch):
                    c = 4 * jb + j
                    ins = e.tensor_copy(
                        hfm[:, c * 512 + itl * 128: c * 512 + (itl + 1) * 128],
                        P.ps["pst"][:, j * 128:(j + 1) * 128])
                return ins

            E.op("vector", f_ev, rd=[("pst", 0)],
                 wr=[("hfm", (itl, jb))])

        for jb in range(NBLK):
            c0 = jb * 512
            bw = min(512, 5504 - c0)
            for which, w in ((0, wg), (1, wu)):
                for hh in range(2):
                    base = which * (DC * 512) + hh * (DC // 2) * bw
                    dst = wbuf[:, base: base + (DC // 2) * bw]
                    src = w[hh * 1024:(hh + 1) * 1024, c0:c0 + bw]
                    E.dma("sp" if which == 0 else "act",
                          dst.rearrange("p (c n) -> p c n", n=bw),
                          src.rearrange("(c p) n -> p c n", p=128),
                          rd=[], wr=[("wbuf", which)])
            for itl in range(4):
                it = tb * 4 + itl
                pp = itl % 2

                def f_gate(e, itl=itl, bw=bw, pp=pp):
                    mm = None
                    for c in range(DC):
                        lhs = t_fm4[:, c * ncols + itl * 128:
                                    c * ncols + (itl + 1) * 128]
                        mm = e.matmul(psg[pp][:, 0:bw], lhs,
                                      wbuf[:, c * bw:(c + 1) * bw],
                                      start=(c == 0), stop=(c == DC - 1))
                    return mm

                def f_up(e, itl=itl, bw=bw, pp=pp):
                    mm = None
                    for c in range(DC):
                        lhs = t_fm4[:, c * ncols + itl * 128:
                                    c * ncols + (itl + 1) * 128]
                        mm = e.matmul(psu[pp][:, 0:bw], lhs,
                                      wbuf[:, DC * 512 + c * bw:
                                           DC * 512 + (c + 1) * bw],
                                      start=(c == 0), stop=(c == DC - 1))
                    return mm

                E.op("tensor", f_up,
                     rd=[("t_fm4", None), ("wbuf", 1)], wr=[("ps_b", pp)])
                E.op("tensor", f_gate,
                     rd=[("t_fm4", None), ("wbuf", 0)], wr=[("ps_a", pp)])
                E.op("scalar", lambda e, bw=bw, pp=pp: e.activation(
                    silu_g[:, pp * 512:pp * 512 + bw], psg[pp][:, 0:bw], AF.Silu),
                    rd=[("ps_a", pp)], wr=[("silu", pp)])
                E.op("vector", lambda e, bw=bw, pp=pp: e.tensor_mul(
                    hb[:, pp * 512:pp * 512 + bw],
                    silu_g[:, pp * 512:pp * 512 + bw], psu[pp][:, 0:bw]),
                    rd=[("silu", pp), ("ps_b", pp)], wr=[("hb", pp)])
                nch = bw // 128
                for fn in pending:
                    fn()
                pending = [lambda jb=jb, itl=itl, nch=nch, pp=pp:
                           emit_tile_transpose(jb, itl, nch, pp)]
        for fn in pending:
            fn()
        E.new_block()
    es2 = contextlib.ExitStack()
    with es2:
        P.sbuf("wdbuf", [128, 2 * 1024], BF16, es2)
        for tt in range(4):
            for db in range(2):
                P.psum(f"ps_dn{tt}_{db}", [128, 512], F32, es2)
        emit_ffn_down(P, x, hfm, wd, tb * 4, IC)
        E.new_block()
        E.fence_all()
        E.new_block()


def build_program(subs):
    P = Prog()
    E = P.E

    x_in = P.dram_in("x_in", [TPC, D], F32)
    x_out = P.dram_out("x_out", [TPC, D], F32)
    P.dram_in("ident_in", [128, 128], F32)
    need = set(subs)

    if need & {"l1f1", "l1f2"}:
        P.dram_in("l1f_wg", [D, 8192], F32)
        P.dram_in("l1f_wu", [D, 8192], F32)
        P.dram_in("l1f_wd", [8192, D], F32)
        P.dram_tmp("l1f_wg_bf", [D, 8192], BF16)
        P.dram_tmp("l1f_wu_bf", [D, 8192], BF16)
        P.dram_tmp("l1f_wd_bf", [8192, D], BF16)
    attn_subs = [s for s in subs if s in ATTN_CFG]
    if attn_subs:
        if DBG_ATTN:
            P.dram_out("dbg_v2", [128, S], F32)
            P.dram_out("dbg_q", [128, S], F32)
            P.dram_out("dbg_k", [128, S], F32)
            P.dram_out("dbg_v", [128, S], F32)
        P.dram_in("mask5_in", [128, 8 * 512], F32)
        P.dram_tmp("t_ag_in", [D, TPC], BF16)
        P.dram_tmp("t_ag", [4 * D, TPC], BF16)
        P.dram_tmp("att_pub", [512, S], BF16)
        P.dram_tmp("att_all", [D, S], BF16)
        P.dram_in("rank_in", [1, 1], I32)
        for tg in set(attn_subs):
            cfg = ATTN_CFG[tg]
            qh_pc = cfg["qh"] // 4
            kv_pc = cfg["kvh"] // 4
            P.dram_in(tg + "_wq_sl", [D, qh_pc * cfg["dh"]], F32)
            P.dram_in(tg + "_wk_sl", [D, kv_pc * cfg["dh"]], F32)
            P.dram_in(tg + "_wv_sl", [D, kv_pc * cfg["dh"]], F32)
            P.dram_in(cfg["wo"], [D, D], F32)
            if cfg.get("rope"):
                P.dram_in(cfg["cos"] + "_in", [128, S], F32)
                P.dram_in(cfg["sin"] + "_in", [128, S], F32)
                P.dram_in(cfg["perm"] + "_in", [128, 128], F32)
    if "l1b" in need:
        P.dram_in("l1b_win", [D, 3 * D], F32)
        P.dram_in("l1b_bin_in", [128, 48], F32)
        P.dram_in("l1b_wout", [D, D], F32)
        P.dram_in("l1b_bout_in", [1, D], F32)
        P.dram_in("l1b_cos_sl", [1024, TPC], F32)
        P.dram_in("l1b_sin_sl", [1024, TPC], F32)
        P.dram_in("l1b_perm_in", [128, 128], F32)
        P.dram_tmp("t_tm_d", [TPC, D], BF16)
        P.dram_tmp("ar_in", [128, 16], F32)
        P.dram_tmp("ar_out", [128, 16], F32)
    if need & {"l0f1", "l0f2"}:
        P.dram_in("lut_in", [1, 16], F32)
        P.dram_in("gidx_sl", [256, 5504], I32)
        P.dram_in("uidx_sl", [256, 5504], I32)
        P.dram_in("didx_sl", [688, 2048], I32)
        P.dram_tmp("wexp_g_in", [256, 5504], BF16)
        P.dram_tmp("wexp_u_in", [256, 5504], BF16)
        P.dram_tmp("wexp_d_in", [688, 2048], BF16)
        P.dram_tmp("wexp_g", [2048, 5504], BF16)
        P.dram_tmp("wexp_u", [2048, 5504], BF16)
        P.dram_tmp("wexp_d", [5504, 2048], BF16)

    with P.es:
        x = P.sbuf("x", [128, NT * D], F32)
        P.sbuf("stats", [128, 8], F32)
        ident = P.sbuf("ident", [128, 128], BF16)
        if attn_subs:
            rank_sb = P.sbuf("rank_sb", [1, 1], I32)
            E.dma("sp", rank_sb[0:1, :], P.dram["rank_in"][:, :],
                  wr=[("rank_sb", None)])

        E.dma("sp", x[:].rearrange("p (n d) -> p n d", d=D),
              x_in[:, :].rearrange("(n p) d -> p n d", p=128),
              wr=[("x", None)])
        E.dma("sw", ident[:], P.dram["ident_in"][:, :], wr=[("ident", None)])

        expand_pending = bool(need & {"l0f1", "l0f2"})
        if expand_pending and "l0a" not in subs:
            emit_expand_l0(P)
            expand_pending = False

        def emit_cast(nm):
            src = P.dram[nm]
            dst = P.dram[nm + "_bf"]
            rows = src.shape[0]
            step = 512
            for r0 in range(0, rows, step):
                E.dma("sw", dst[r0:r0 + step, :], src[r0:r0 + step, :],
                      wr=[(nm + "_bf", r0 // step)])

        cast_after = {}
        if need & {"l1f1", "l1f2"}:
            order = [s for s in subs if s in ("l0a", "l0b", "l1a")]
            todo = ["l1f_wg", "l1f_wu", "l1f_wd"]
            for i, nm in enumerate(todo):
                key = order[i] if i < len(order) else (subs[0] if order == [] else order[-1])
                cast_after.setdefault(key, []).append(nm)
            if not order:
                for nm in todo:
                    emit_cast(nm)
                cast_after = {}
        E.new_block()

        for sub in subs:
            E.rotate_qsems(sub)
            if sub in ("l1f1", "l1f2"):
                E.fence_all("gpsimd")
                E.new_block()
                emit_sub_l1f(P, sub, P.dram["l1f_wg_bf"], P.dram["l1f_wu_bf"],
                             P.dram["l1f_wd_bf"])
            elif sub in ("l0f1", "l0f2"):
                emit_sub_l0f(P, sub)
            elif sub in ATTN_CFG:
                hook = None
                if sub == "l0a" and expand_pending:
                    hook = emit_expand_l0
                    expand_pending = False
                emit_sub_attn(P, sub, post_hook=hook)
            elif sub == "l1b":
                emit_sub_l1b(P, sub)
            else:
                raise NotImplementedError(sub)
            for nm in cast_after.get(sub, ()):
                emit_cast(nm)

        E.dma("sp", x_out[:, :].rearrange("(n p) d -> p n d", p=128),
              x[:].rearrange("p (n d) -> p n d", d=D), rd=[("x", None)])
        E.fence_all("sync")
        E.realize()
    return P


def _shard_x(xfull):
    xf = np.ascontiguousarray(np.asarray(xfull).reshape(B * S, D))
    return [np.ascontiguousarray(xf[c * TPC:(c + 1) * TPC]) for c in range(NCORES)]


def make_in_maps(inputs, subs, x_cur, w=None, c=None):
    w = w if w is not None else host_prepare(inputs)
    c = c if c is not None else host_consts()
    shards = _shard_x(x_cur)
    need = set(subs)
    in_maps = []
    for core in range(NCORES):
        m = {"x_in": shards[core], "ident_in": c["ident"]}
        if need & {"l1f1", "l1f2"}:
            m["l1f_wg"] = w["l1f_wg"]
            m["l1f_wu"] = w["l1f_wu"]
            m["l1f_wd"] = w["l1f_wd"]
        for tg in need & set(ATTN_CFG):
            cfg = ATTN_CFG[tg]
            qh_pc = cfg["qh"] // 4
            kv_pc = cfg["kvh"] // 4
            g = core % 4
            m["mask5_in"] = c["mask5"]
            m["rank_in"] = np.array([[core % 4]], np.int32)
            m[tg + "_wq_sl"] = np.ascontiguousarray(
                w[cfg["wq"]][:, g * qh_pc * cfg["dh"]:(g + 1) * qh_pc * cfg["dh"]])
            m[tg + "_wk_sl"] = np.ascontiguousarray(
                w[cfg["wk"]][:, g * kv_pc * cfg["dh"]:(g + 1) * kv_pc * cfg["dh"]])
            m[tg + "_wv_sl"] = np.ascontiguousarray(
                w[cfg["wv"]][:, g * kv_pc * cfg["dh"]:(g + 1) * kv_pc * cfg["dh"]])
            m[cfg["wo"]] = w[cfg["wo"]]
            if cfg.get("rope"):
                m[cfg["cos"] + "_in"] = c[cfg["cos"]]
                m[cfg["sin"] + "_in"] = c[cfg["sin"]]
                m[cfg["perm"] + "_in"] = c[cfg["perm"]]
        if "l1b" in need:
            tok0 = (core % 4) * TPC
            m["l1b_win"] = w["l1b_win"]
            m["l1b_bin_in"] = w["l1b_bin"]
            m["l1b_wout"] = w["l1b_wout"]
            m["l1b_bout_in"] = w["l1b_bout"]
            m["l1b_cos_sl"] = np.ascontiguousarray(c["l1b_cosT"][:, tok0:tok0 + TPC])
            m["l1b_sin_sl"] = np.ascontiguousarray(c["l1b_sinT"][:, tok0:tok0 + TPC])
            m["l1b_perm_in"] = c["l1b_permT"]
        if need & {"l0f1", "l0f2"}:
            m["lut_in"] = w["l0f_lut"]
            m["gidx_sl"] = np.ascontiguousarray(
                w["l0f_gidxT"][core * 256:(core + 1) * 256])
            m["uidx_sl"] = np.ascontiguousarray(
                w["l0f_uidxT"][core * 256:(core + 1) * 256])
            m["didx_sl"] = np.ascontiguousarray(
                w["l0f_didxT"][core * 688:(core + 1) * 688])
        in_maps.append(m)
    return in_maps


def run_subset(inputs, subs, x_cur=None, w=None, c=None):
    if x_cur is None:
        x_cur = inputs["L0_attn_a_hidden_states"]
    P = build_program(subs)
    in_maps = make_in_maps(inputs, subs, x_cur, w=w, c=c)
    res = run_bass_kernel_spmd(P.nc, in_maps, core_ids=list(range(NCORES)))
    out = np.concatenate([res.results[cc]["x_out"] for cc in range(NCORES)], axis=0)
    return np.ascontiguousarray(out.reshape(B, S, D)).astype(np.float32)


ALL_SUBS = ["l0a", "l0f1", "l0b", "l0f2", "l1a", "l1f1", "l1b", "l1f2"]


def kernel(**inputs):
    return run_subset(inputs, ALL_SUBS)


# version 11
# speedup vs baseline: 1.0391x; 1.0391x over previous
import contextlib
from collections import defaultdict

import numpy as np
import ml_dtypes
import concourse.bass as bass
import concourse.mybir as mybir
from concourse.bass_utils import run_bass_kernel_spmd

F32 = mybir.dt.float32
BF16 = mybir.dt.bfloat16
I32 = mybir.dt.int32
AF = mybir.ActivationFunctionType
ALU = mybir.AluOpType
AX = mybir.AxisListType

BFNP = np.dtype(ml_dtypes.bfloat16)

B, S, D = 2, 4096, 2048
NCORES = 8
TPC = 1024
NT = TPC // 128
DC = D // 128
EPS = 1e-5

ENGINES = ("tensor", "vector", "scalar", "gpsimd", "sync")
ESEM = {"tensor": "pe", "vector": "dv", "scalar": "ac", "gpsimd": "pl"}
QENG = {"sp": "sync", "act": "scalar", "sw": "gpsimd"}

DBG_ATTN = False
GROUPS_B = [[0, 1, 2, 3], [4, 5, 6, 7]]
GROUP_ALL = [[0, 1, 2, 3, 4, 5, 6, 7]]


class Em:
    def __init__(self, nc, es):
        self.nc = nc
        self.es = es
        self.blocks = []
        self.cur = {e: [] for e in ENGINES}
        self.cur_has = False
        self.sems = {}
        self.cnt = defaultdict(int)
        self.hw = {e: defaultdict(int) for e in ENGINES}
        self.writes = defaultdict(list)
        self.reads = defaultdict(list)
        self.KRING = 24
        self.qring = {q: [f"d{q}_{i}" for i in range(self.KRING)]
                      for q in ("sp", "act", "sw")}
        self.qidx = {q: 0 for q in ("sp", "act", "sw")}
        self.dma_sems = {s for ring in self.qring.values() for s in ring} | {"cc"}

    def sem(self, name):
        if name not in self.sems:
            self.sems[name] = self.es.enter_context(self.nc.semaphore(name))
        return self.sems[name]

    def rotate_qsems(self, tag):
        pass

    def new_block(self):
        if not self.cur_has:
            return
        self.blocks.append(self.cur)
        self.cur = {e: [] for e in ENGINES}
        self.cur_has = False
        for w in (self.writes, self.reads):
            for name in list(w):
                w[name] = [r for r in w[name] if r[1] in self.dma_sems]
                if not w[name]:
                    del w[name]
        allsems = set()
        for e in ENGINES:
            allsems |= set(self.hw[e])
        for s in allsems:
            m = max(self.hw[e][s] for e in ENGINES)
            for e in ENGINES:
                self.hw[e][s] = m

    @staticmethod
    def _cf(a, b):
        return a is None or b is None or a == b

    def _deps(self, rd, wr):
        need = {}

        def add(sem, val):
            if need.get(sem, 0) < val:
                need[sem] = val

        for t, slot in rd:
            for slot2, sem, val in self.writes.get(t, ()):
                if self._cf(slot, slot2):
                    add(sem, val)
        for t, slot in wr:
            for slot2, sem, val in self.writes.get(t, ()):
                if self._cf(slot, slot2):
                    add(sem, val)
            for slot2, sem, val in self.reads.get(t, ()):
                if self._cf(slot, slot2):
                    add(sem, val)
        return need

    def _filter(self, eng, need):
        hw = self.hw[eng]
        waits = [(s, v) for s, v in need.items() if hw[s] < v]
        for s, v in waits:
            hw[s] = v
        return [(self.sem(s), v) for s, v in waits]

    def _record(self, rd, wr, sem, val):
        for t, slot in wr:
            lst = self.writes[t]
            if slot is not None:
                lst[:] = [r for r in lst if r[0] != slot]
            else:
                lst.clear()
            lst.append((slot, sem, val))
            rl = self.reads.get(t)
            if rl is not None:
                if slot is None:
                    rl.clear()
                else:
                    rl[:] = [r for r in rl if r[0] != slot]
        for t, slot in rd:
            lst = self.reads[t]
            if len(lst) > 128:
                old, keep = lst[:64], lst[64:]
                agg = {}
                for s2, sm, v in old:
                    if agg.get(sm, 0) < v:
                        agg[sm] = v
                lst[:] = [(None, sm, v) for sm, v in agg.items()] + keep
            lst.append((slot, sem, val))

    def op(self, eng, fn, rd=(), wr=()):
        need = self._deps(rd, wr)
        sem_name = ESEM[eng]
        self.cnt[sem_name] += 1
        val = self.cnt[sem_name]
        wait_hs = self._filter(eng, need)
        sem_h = self.sem(sem_name)

        def run(e, fn=fn, wait_hs=wait_hs, sem_h=sem_h):
            for sh, v in wait_hs:
                e.wait_ge(sh, v)
            fn(e).then_inc(sem_h, 1)

        self.cur[eng].append(run)
        self.cur_has = True
        self._record(rd, wr, sem_name, val)
        return val

    @staticmethod
    def _dram_name(ap):
        t = getattr(ap, "tensor", None)
        if t is not None and type(t).__name__ == "DRamTensorHandle":
            return t.name
        return None

    def dma(self, q, out, in_, rd=(), wr=(), **kw):
        eng = QENG[q]
        sem_name = self.qring[q][self.qidx[q] % self.KRING]
        self.qidx[q] += 1
        rd = list(rd)
        wr = list(wr)
        declared = {t for t, _ in rd} | {t for t, _ in wr}
        nm = self._dram_name(in_)
        if nm is not None and nm not in declared:
            rd.append((nm, None))
        nm = self._dram_name(out)
        if nm is not None and nm not in declared:
            wr.append((nm, None))
        need = self._deps(rd, wr)
        prev = self.cnt[sem_name]
        if prev > 0 and need.get(sem_name, 0) < prev:
            need[sem_name] = prev
        self.cnt[sem_name] += 16
        val = self.cnt[sem_name]
        wait_hs = self._filter(eng, need)
        sem_h = self.sem(sem_name)

        def run(e, out=out, in_=in_, wait_hs=wait_hs, sem_h=sem_h, kw=kw):
            for sh, v in wait_hs:
                e.wait_ge(sh, v)
            e.dma_start(out=out, in_=in_, **kw).then_inc(sem_h, 16)

        self.cur[eng].append(run)
        self.cur_has = True
        self._record(rd, wr, sem_name, val)
        return sem_name, val

    def dma_custom(self, q, fn, rd=(), wr=()):
        eng = QENG[q]
        sem_name = self.qring[q][self.qidx[q] % self.KRING]
        self.qidx[q] += 1
        need = self._deps(rd, wr)
        prev = self.cnt[sem_name]
        if prev > 0 and need.get(sem_name, 0) < prev:
            need[sem_name] = prev
        self.cnt[sem_name] += 16
        val = self.cnt[sem_name]
        wait_hs = self._filter(eng, need)
        sem_h = self.sem(sem_name)

        def run(e, fn=fn, wait_hs=wait_hs, sem_h=sem_h):
            for sh, v in wait_hs:
                e.wait_ge(sh, v)
            ins = fn(e)
            ins.then_inc(sem_h, 16)

        self.cur[eng].append(run)
        self.cur_has = True
        self._record(rd, wr, sem_name, val)
        return sem_name, val

    def collective(self, kind, op, groups, ins_ap, outs_ap, rd=(), wr=()):
        need = self._deps(rd, wr)
        self.cnt["cc"] += 1
        val = self.cnt["cc"]
        wait_hs = self._filter("gpsimd", need)
        sem_h = self.sem("cc")

        def run(e, wait_hs=wait_hs, sem_h=sem_h):
            for sh, v in wait_hs:
                e.wait_ge(sh, v)
            e.collective_compute(kind, op, replica_groups=groups,
                                 ins=ins_ap, outs=outs_ap).then_inc(sem_h)

        self.cur["gpsimd"].append(run)
        self.cur_has = True
        self._record(rd, wr, "cc", val)
        return val

    def raw_wait(self, eng, sem_name, val):
        if self.hw[eng][sem_name] >= val:
            return
        self.hw[eng][sem_name] = val
        sh = self.sem(sem_name)
        self.cur[eng].append(lambda e, sh=sh, v=val: e.wait_ge(sh, v))
        self.cur_has = True

    def fence_all(self, eng="gpsimd"):
        for s in sorted(self.dma_sems):
            if self.cnt[s] > 0:
                self.raw_wait(eng, s, self.cnt[s])

    def realize(self):
        self.new_block()
        nc = self.nc
        for blk_ops in self.blocks:
            with nc.Block() as blk:
                for ename in ENGINES:
                    lst = blk_ops[ename]
                    if lst:
                        getattr(blk, ename)(lambda e, lst=lst: [f(e) for f in lst])


def rope_tables_interleaved(rot_dim, positions):
    inv = 1.0 / (10000.0 ** (np.arange(0, rot_dim, 2, dtype=np.float64) / rot_dim))
    fr = positions[:, None].astype(np.float64) * inv[None, :]
    cos = np.repeat(np.cos(fr), 2, axis=1)
    sin = np.repeat(np.sin(fr), 2, axis=1)
    return cos.astype(np.float32), sin.astype(np.float32)


def rope_tables_half(d_head, positions):
    inv = 1.0 / (10000.0 ** (np.arange(0, d_head, 2, dtype=np.float64) / d_head))
    fr = positions[:, None].astype(np.float64) * inv[None, :]
    emb = np.concatenate([fr, fr], axis=1)
    return np.cos(emb).astype(np.float32), np.sin(emb).astype(np.float32)


def perm_interleaved(n):
    P = np.zeros((n, n), np.float32)
    for i in range(n // 2):
        P[2 * i, 2 * i + 1] = -1.0
        P[2 * i + 1, 2 * i] = 1.0
    return P


def perm_half(n):
    P = np.zeros((n, n), np.float32)
    h = n // 2
    for i in range(h):
        P[i, i + h] = -1.0
        P[i + h, i] = 1.0
    return P


def host_prepare(inputs):
    d = {}
    w_qkv = inputs["L0_attn_a_w_qkv"]
    local = 512
    rq, rv, rk = [], [], []
    for m in range(4):
        base = m * 3 * local
        rq.append(np.arange(base, base + local))
        rv.append(np.arange(base + local, base + 2 * local))
        rk.append(np.arange(base + 2 * local, base + 3 * local))
    d["l0a_wq"] = np.ascontiguousarray(w_qkv[np.concatenate(rq)].T)
    d["l0a_wk"] = np.ascontiguousarray(w_qkv[np.concatenate(rk)].T)
    d["l0a_wv"] = np.ascontiguousarray(w_qkv[np.concatenate(rv)].T)
    d["l0a_wo"] = np.ascontiguousarray(inputs["L0_attn_a_w_out"].T)
    d["l0b_wq"] = np.ascontiguousarray(inputs["L0_attn_b_Wq"])
    d["l0b_wk"] = np.ascontiguousarray(inputs["L0_attn_b_Wk"])
    d["l0b_wv"] = np.ascontiguousarray(inputs["L0_attn_b_Wv"])
    d["l0b_wo"] = np.ascontiguousarray(inputs["L0_attn_b_Wo"])
    d["l0f_gidxT"] = np.ascontiguousarray(inputs["L0_ffn_gate_idx"].T)
    d["l0f_uidxT"] = np.ascontiguousarray(inputs["L0_ffn_up_idx"].T)
    d["l0f_didxT"] = np.ascontiguousarray(inputs["L0_ffn_down_idx"].T)
    d["l0f_lut"] = np.ascontiguousarray(inputs["L0_ffn_lut"].reshape(1, 16))
    d["l1a_wq"] = np.ascontiguousarray(inputs["L1_attn_a_wq"])
    d["l1a_wk"] = np.ascontiguousarray(inputs["L1_attn_a_wk"])
    d["l1a_wv"] = np.ascontiguousarray(inputs["L1_attn_a_wv"])
    d["l1a_wo"] = np.ascontiguousarray(inputs["L1_attn_a_wo"])
    d["l1b_win"] = np.ascontiguousarray(inputs["L1_attn_b_W_in"].T)
    d["l1b_bin"] = np.ascontiguousarray(
        inputs["L1_attn_b_b_in"].reshape(48, 128).T)
    d["l1b_wout"] = np.ascontiguousarray(inputs["L1_attn_b_W_out"].T)
    d["l1b_bout"] = np.ascontiguousarray(inputs["L1_attn_b_b_out"].reshape(1, -1))
    for nm, key in (("l1f_wg", "L1_ffn_w_gate"), ("l1f_wu", "L1_ffn_w_up")):
        A = np.asarray(inputs[key]).reshape(2, 8, 128, 16, 512)
        d[nm] = np.ascontiguousarray(
            A.transpose(3, 0, 2, 1, 4).reshape(4096, 4096))
    B = np.asarray(inputs["L1_ffn_w_down"]).reshape(32, 2, 128, 2, 1024)
    d["l1f_wd"] = np.ascontiguousarray(
        B.transpose(3, 0, 2, 1, 4).reshape(8192, 2048))
    return d


def host_consts():
    c = {}
    c["ident"] = np.eye(128, dtype=np.float32)
    k = np.arange(128)[:, None]
    q = np.arange(128)[None, :]
    c["tri_kq"] = (k <= q).astype(np.float32)
    c["tri_ge_kq"] = (k >= q).astype(np.float32)
    kk = np.arange(128)[:, None]
    qq = np.arange(512)[None, :]
    m8 = np.zeros((128, 8 * 512), np.float32)
    for d in range(4):
        m8[:, d * 512:(d + 1) * 512] = (kk + d * 128 <= qq)
    for ei in range(4):
        m8[:, (4 + ei) * 512:(5 + ei) * 512] = (kk >= qq - ei * 128)
    c["mask5"] = m8
    pos = np.arange(S)
    cos64, sin64 = rope_tables_interleaved(64, pos)
    t = np.ones((128, S), np.float32)
    z = np.zeros((128, S), np.float32)
    t[0:64] = cos64.T
    z[0:64] = sin64.T
    c["l0a_cos"], c["l0a_sin"] = t, z
    cosh, sinh = rope_tables_half(64, pos)
    c["l1a_cos"] = np.ascontiguousarray(
        np.concatenate([cosh.T, cosh.T], axis=0)).astype(np.float32)
    c["l1a_sin"] = np.ascontiguousarray(
        np.concatenate([sinh.T, sinh.T], axis=0)).astype(np.float32)
    cosb, sinb = rope_tables_interleaved(1024, pos)
    c["l1b_cosT"] = np.ascontiguousarray(cosb.T)
    c["l1b_sinT"] = np.ascontiguousarray(sinb.T)
    P64 = np.zeros((128, 128), np.float32)
    P64[0:64, 0:64] = perm_interleaved(64)
    c["l0a_permT"] = np.ascontiguousarray(P64.T)
    Ph = np.zeros((128, 128), np.float32)
    Ph[0:64, 0:64] = perm_half(64)
    Ph[64:128, 64:128] = perm_half(64)
    c["l1a_permT"] = np.ascontiguousarray(Ph.T)
    c["l1b_permT"] = np.ascontiguousarray(perm_interleaved(128).T)
    return c


class Prog:
    def __init__(self):
        self.nc = bass.Bass("TRN2", target_bir_lowering=False)
        self.es = contextlib.ExitStack()
        self.E = Em(self.nc, self.es)
        self.dram = {}
        self.sb = {}
        self.ps = {}
        self._uniq = 0

    def dram_in(self, name, shape, dtype=F32):
        t = self.nc.dram_tensor(name, list(shape), dtype, kind="ExternalInput")
        self.dram[name] = t
        return t

    def dram_out(self, name, shape, dtype=F32):
        t = self.nc.dram_tensor(name, list(shape), dtype, kind="ExternalOutput")
        self.dram[name] = t
        return t

    def dram_tmp(self, name, shape, dtype=BF16):
        t = self.nc.dram_tensor(name, list(shape), dtype)
        self.dram[name] = t
        return t

    def sbuf(self, name, shape, dtype, es=None):
        self._uniq += 1
        t = (es or self.es).enter_context(
            self.nc.sbuf_tensor(f"{name}_{self._uniq}", list(shape), dtype))
        self.sb[name] = t
        return t

    def psum(self, name, shape, dtype=F32, es=None):
        self._uniq += 1
        t = (es or self.es).enter_context(
            self.nc.psum_tensor(f"{name}_{self._uniq}", list(shape), dtype))
        self.ps[name] = t
        return t


def emit_ln_tile(P, x, it, t_out, t_out_name, t_out_off=0):
    E = P.E
    stats = P.sb["stats"]
    xs = x[:, it * D:(it + 1) * D]
    ssum = stats[:, 0:1]
    ssq = stats[:, 1:2]
    mean = stats[:, 2:3]
    va = stats[:, 3:4]
    m2 = stats[:, 4:5]
    rstd = stats[:, 5:6]
    to = t_out[:, t_out_off:t_out_off + D]

    tslot = (t_out_name, t_out_off)
    E.op("scalar", lambda e: e.activation(to, xs, AF.Copy, accum_out=ssum),
         rd=[("x", it)], wr=[("st", 0), tslot])
    E.op("scalar", lambda e: e.activation(to, xs, AF.Square, accum_out=ssq),
         rd=[("x", it)], wr=[("st", 1), tslot])
    E.op("vector", lambda e: e.tensor_scalar(mean, ssum, 1.0 / D, None, ALU.mult),
         rd=[("st", 0)], wr=[("st", 2)])
    E.op("vector", lambda e: e.tensor_scalar(va, ssq, 1.0 / D, EPS, ALU.mult, ALU.add),
         rd=[("st", 1)], wr=[("st", 3)])
    E.op("vector", lambda e: e.tensor_mul(m2, mean, mean),
         rd=[("st", 2)], wr=[("st", 4)])
    E.op("vector", lambda e: e.tensor_sub(va, va, m2),
         rd=[("st", 3), ("st", 4)], wr=[("st", 3)])
    E.op("scalar", lambda e: e.activation(va, va, AF.Sqrt),
         rd=[("st", 3)], wr=[("st", 3)])
    E.op("vector", lambda e: e.reciprocal(rstd, va),
         rd=[("st", 3)], wr=[("st", 5)])
    E.op("vector", lambda e: e.tensor_scalar(to, xs, mean, rstd,
                                             ALU.subtract, ALU.mult),
         rd=[("x", it), ("st", 2), ("st", 5)], wr=[tslot])


def emit_transpose_tile(P, src, src_name, src_slot, dst, dst_name, col_off,
                        ncols, dc=DC, src_off=0):
    E = P.E
    ident = P.sb["ident"]
    pst = P.ps["pst"]
    for cg in range(0, dc, 8):
        n = min(8, dc - cg)

        def f_tr(e, cg=cg, n=n):
            mm = None
            for j in range(n):
                c = cg + j
                mm = e.matmul(pst[:, j * 128:(j + 1) * 128],
                              src[:, src_off + c * 128:src_off + (c + 1) * 128],
                              ident[:], is_transpose=True,
                              start=(j == 0), stop=(j == n - 1))
            return mm

        E.op("tensor", f_tr, rd=[(src_name, src_slot), ("ident", None)],
             wr=[("pst", 0)])

        def f_ev(e, cg=cg, n=n):
            ins = None
            for j in range(n):
                c = cg + j
                ins = e.tensor_copy(
                    dst[:, c * ncols + col_off: c * ncols + col_off + 128],
                    pst[:, j * 128:(j + 1) * 128])
            return ins

        E.op("vector", f_ev, rd=[("pst", 0)],
             wr=[(dst_name, (col_off, cg))])


def emit_sparsify(P, h, n, out, out_name, h_dep):
    E = P.E
    habs = P.sb["habs"]
    sp4 = P.sb["sp4"]
    q = n // 4
    A = habs[:, 0:n]
    m01 = sp4[:, 0 * q:1 * q]
    n01 = sp4[:, 1 * q:2 * q]
    m23 = sp4[:, 2 * q:3 * q]
    n23 = sp4[:, 3 * q:4 * q]
    dge = sp4[:, 4 * q:5 * q]

    def av(i):
        return A[:, i::4]

    E.op("scalar", lambda e: e.activation(A, h, AF.Abs),
         rd=[h_dep], wr=[("habs", None)])
    E.op("vector", lambda e: e.tensor_tensor(m01, av(0), av(1), ALU.max),
         rd=[("habs", None)], wr=[("sp4", 0)])
    E.op("vector", lambda e: e.tensor_tensor(n01, av(0), av(1), ALU.min),
         rd=[("habs", None)], wr=[("sp4", 1)])
    E.op("vector", lambda e: e.tensor_tensor(m23, av(2), av(3), ALU.max),
         rd=[("habs", None)], wr=[("sp4", 2)])
    E.op("vector", lambda e: e.tensor_tensor(n23, av(2), av(3), ALU.min),
         rd=[("habs", None)], wr=[("sp4", 3)])
    E.op("vector", lambda e: e.tensor_tensor(dge, m01, m23, ALU.is_ge),
         rd=[("sp4", 0), ("sp4", 2)], wr=[("sp4", 4)])
    E.op("vector", lambda e: e.tensor_sub(n01, n01, n23),
         rd=[("sp4", 1), ("sp4", 3)], wr=[("sp4", 1)])
    E.op("vector", lambda e: e.tensor_mul(n01, n01, dge),
         rd=[("sp4", 1), ("sp4", 4)], wr=[("sp4", 1)])
    E.op("vector", lambda e: e.tensor_add(n01, n01, n23),
         rd=[("sp4", 1), ("sp4", 3)], wr=[("sp4", 1)])
    E.op("vector", lambda e: e.tensor_tensor(m01, m01, m23, ALU.min),
         rd=[("sp4", 0), ("sp4", 2)], wr=[("sp4", 0)])
    E.op("vector", lambda e: e.tensor_tensor(m01, m01, n01, ALU.max),
         rd=[("sp4", 0), ("sp4", 1)], wr=[("sp4", 0)])
    for i in range(4):
        E.op("vector", lambda e, i=i: e.tensor_tensor(dge, av(i), m01, ALU.is_ge),
             rd=[("habs", None), ("sp4", 0)], wr=[("sp4", 4)])
        E.op("vector", lambda e, i=i: e.tensor_mul(out[:, i::4], h[:, i::4], dge),
             rd=[h_dep, ("sp4", 4)], wr=[(out_name, i)])


def emit_sub_l1f(P, tag, wg, wu, wd):
    E = P.E
    x = P.sb["x"]
    IC = 8192 // 128
    NB = 8192 // 512
    ident = P.sb["ident"]

    for tb in range(NT // 4):
        with contextlib.ExitStack() as es_tb:
            hfm = P.sbuf("hfm", [128, IC * 512], BF16, es_tb)
            _l1f_block(P, tag, wg, wu, wd, tb, hfm)


def _l1f_block(P, tag, wg, wu, wd, tb, hfm):
    E = P.E
    x = P.sb["x"]
    IC = 8192 // 128
    NB = 8192 // 512
    ident = P.sb["ident"]
    if True:
        es = contextlib.ExitStack()
        with es:
            t_stage = P.sbuf("t_stage", [128, D], BF16, es)
            t_fm4 = P.sbuf("t_fm4", [128, DC * 512], BF16, es)
            wbuf = P.sbuf("wbuf", [128, 2 * DC * 512], BF16, es)
            h_tm = P.sbuf("h_tm", [128, 1024], F32, es)
            P.sbuf("habs", [128, 1024], F32, es)
            P.sbuf("sp4", [128, 5 * 256], F32, es)
            hsp = P.sbuf("hsp", [128, 2048], BF16, es)
            silu_g = P.sbuf("silu_g", [128, 2 * 512], BF16, es)
            psg = [P.psum("ps_a0", [128, 512], F32, es),
                   P.psum("ps_a1", [128, 512], F32, es)]
            psu = [P.psum("ps_b0", [128, 512], F32, es),
                   P.psum("ps_b1", [128, 512], F32, es)]
            P.psum("pst", [128, 1024], BF16, es)

            for itl in range(4):
                it = tb * 4 + itl
                emit_ln_tile(P, x, it, t_stage, "t_stage")
                emit_transpose_tile(P, t_stage, "t_stage", 0,
                                    t_fm4, "t_fm4", itl * 128, 512)
            E.new_block()

            ncols = 512
            pending = []

            def emit_pair_transpose(jb, itl, par):
                hnm = f"hsp{par}"
                for sub_i in range(2):
                    itl2 = itl - 1 + sub_i

                    def f_tr(e, sub_i=sub_i, par=par):
                        mm = None
                        for j in range(4):
                            mm = e.matmul(
                                P.ps["pst"][:, j * 128:(j + 1) * 128],
                                hsp[:, par * 1024 + sub_i * 512 + j * 128:
                                    par * 1024 + sub_i * 512 + (j + 1) * 128],
                                ident[:], is_transpose=True,
                                start=(j == 0), stop=(j == 3))
                        return mm

                    E.op("tensor", f_tr, rd=[(hnm, None), ("ident", None)],
                         wr=[("pst", 0)])

                    def f_ev(e, itl2=itl2, jb=jb):
                        ins = None
                        for j in range(4):
                            c = 4 * jb + j
                            ins = e.tensor_copy(
                                hfm[:, c * 512 + itl2 * 128:
                                    c * 512 + (itl2 + 1) * 128],
                                P.ps["pst"][:, j * 128:(j + 1) * 128])
                        return ins

                    E.op("vector", f_ev, rd=[("pst", 0)],
                         wr=[("hfm", (itl2, jb))])

            for jb in range(NB):
                for which, w in ((0, wg), (1, wu)):
                    for hh in range(2):
                        dst = wbuf[:, which * (DC * 512) + hh * (DC // 2) * 512:
                                   which * (DC * 512) + (hh + 1) * (DC // 2) * 512]
                        src = w[(jb * 2 + hh) * 128:(jb * 2 + hh + 1) * 128, :]
                        E.dma("sp" if which == 0 else "act", dst, src,
                              rd=[], wr=[("wbuf", which)])
                for itl in range(4):
                    pp = itl % 2

                    def f_gate(e, itl=itl, pp=pp):
                        mm = None
                        for c in range(DC):
                            lhs = t_fm4[:, c * ncols + itl * 128:
                                        c * ncols + (itl + 1) * 128]
                            mm = e.matmul(psg[pp][:], lhs,
                                          wbuf[:, c * 512:(c + 1) * 512],
                                          start=(c == 0), stop=(c == DC - 1))
                        return mm

                    def f_up(e, itl=itl, pp=pp):
                        mm = None
                        for c in range(DC):
                            lhs = t_fm4[:, c * ncols + itl * 128:
                                        c * ncols + (itl + 1) * 128]
                            mm = e.matmul(psu[pp][:], lhs,
                                          wbuf[:, DC * 512 + c * 512:
                                               DC * 512 + (c + 1) * 512],
                                          start=(c == 0), stop=(c == DC - 1))
                        return mm

                    E.op("tensor", f_up,
                         rd=[("t_fm4", None), ("wbuf", 1)], wr=[("ps_b", pp)])
                    E.op("tensor", f_gate,
                         rd=[("t_fm4", None), ("wbuf", 0)], wr=[("ps_a", pp)])
                    E.op("scalar", lambda e, pp=pp: e.activation(
                        silu_g[:, pp * 512:(pp + 1) * 512], psg[pp][:], AF.Silu),
                        rd=[("ps_a", pp)], wr=[("silu", pp)])
                    E.op("vector", lambda e, itl=itl, pp=pp: e.tensor_mul(
                        h_tm[:, pp * 512:(pp + 1) * 512],
                        silu_g[:, pp * 512:(pp + 1) * 512], psu[pp][:]),
                        rd=[("silu", pp), ("ps_b", pp)], wr=[("h_tm", pp)])
                    if itl % 2 == 0:
                        continue
                    par = (jb * 2 + itl // 2) % 2
                    emit_sparsify(P, h_tm[:, 0:1024], 1024,
                                  hsp[:, par * 1024:par * 1024 + 1024],
                                  f"hsp{par}", ("h_tm", None))
                    for fn in pending:
                        fn()
                    pending = [lambda jb=jb, itl=itl, par=par:
                               emit_pair_transpose(jb, itl, par)]
            for fn in pending:
                fn()
            E.new_block()
        es2 = contextlib.ExitStack()
        with es2:
            wdbuf = P.sbuf("wdbuf", [128, 4 * 2048], BF16, es2)
            for tt in range(4):
                for db in range(2):
                    P.psum(f"ps_dn{tt}_{db}", [128, 512], F32, es2)
            hfm = P.sb["hfm"]
            emit_ffn_down_l1(P, x, hfm, wd, tb * 4)
            E.new_block()
            E.fence_all()
            E.new_block()


def emit_ffn_down_l1(P, x, hfm, wd, it0):
    E = P.E
    wdbuf = P.sb["wdbuf"]
    DQ = ("sp", "act", "sp", "act")
    for dhalf in range(2):
        for icp in range(32):
            slot = icp % 4
            E.dma(DQ[slot], wdbuf[:, slot * 2048:(slot + 1) * 2048],
                  wd[(dhalf * 32 + icp) * 128:(dhalf * 32 + icp + 1) * 128, :],
                  rd=[], wr=[("wdbuf", slot)])
            for tt in range(4):
                def f_mm(e, icp=icp, tt=tt, slot=slot):
                    mm = None
                    for sub in range(2):
                        ic = icp * 2 + sub
                        lhs = hfm[:, ic * 512 + tt * 128:
                                  ic * 512 + (tt + 1) * 128]
                        for db in range(2):
                            mm = e.matmul(
                                P.ps[f"ps_dn{tt}_{db}"][:], lhs,
                                wdbuf[:, slot * 2048 + sub * 1024 + db * 512:
                                      slot * 2048 + sub * 1024 + (db + 1) * 512],
                                start=(icp == 0 and sub == 0),
                                stop=(icp == 31 and sub == 1),
                                skip_group_check=True)
                    return mm

                E.op("tensor", f_mm,
                     rd=[("hfm", None), ("wdbuf", slot)],
                     wr=[("ps_dn", (tt, 0)), ("ps_dn", (tt, 1))])
        for tt in range(4):
            it = it0 + tt
            for db in range(2):
                def f_ev(e, it=it, tt=tt, db=db, dhalf=dhalf):
                    xs = x[:, it * D + dhalf * 1024 + db * 512:
                           it * D + dhalf * 1024 + (db + 1) * 512]
                    return e.tensor_add(xs, xs, P.ps[f"ps_dn{tt}_{db}"][:])

                E.op("vector", f_ev,
                     rd=[("ps_dn", (tt, db)), ("x", it)], wr=[("x", it)])


def emit_ffn_down(P, x, hfm, wd, it0, IC):
    E = P.E
    wdbuf = P.sb["wdbuf"]
    for dhalf in range(2):
        for ic in range(IC):
            wslot = ic % 2
            E.dma("act" if wslot == 0 else "sp",
                  wdbuf[:, wslot * 1024:(wslot + 1) * 1024],
                  wd[ic * 128:(ic + 1) * 128, dhalf * 1024:(dhalf + 1) * 1024],
                  rd=[], wr=[("wdbuf", wslot)])
            for tt in range(4):
                def f_mm(e, ic=ic, tt=tt, wslot=wslot):
                    mm = None
                    lhs = hfm[:, ic * 512 + tt * 128: ic * 512 + (tt + 1) * 128]
                    for db in range(2):
                        mm = e.matmul(
                            P.ps[f"ps_dn{tt}_{db}"][:], lhs,
                            wdbuf[:, wslot * 1024 + db * 512:
                                  wslot * 1024 + (db + 1) * 512],
                            start=(ic == 0), stop=(ic == IC - 1),
                            skip_group_check=True)
                    return mm

                E.op("tensor", f_mm,
                     rd=[("hfm", None), ("wdbuf", wslot)],
                     wr=[("ps_dn", (tt, 0)), ("ps_dn", (tt, 1))])
        for tt in range(4):
            it = it0 + tt
            for db in range(2):
                def f_ev(e, it=it, tt=tt, db=db, dhalf=dhalf):
                    xs = x[:, it * D + dhalf * 1024 + db * 512:
                           it * D + dhalf * 1024 + (db + 1) * 512]
                    return e.tensor_add(xs, xs, P.ps[f"ps_dn{tt}_{db}"][:])

                E.op("vector", f_ev,
                     rd=[("ps_dn", (tt, db)), ("x", it)], wr=[("x", it)])


def emit_attn_prologue(P):
    E = P.E
    x = P.sb["x"]
    with contextlib.ExitStack() as es:
        t_stage = P.sbuf("t_stage", [128, D], BF16, es)
        t_fml = P.sbuf("t_fml", [128, DC * 1024], BF16, es)
        P.psum("pst", [128, 1024], BF16, es)
        for it in range(NT):
            emit_ln_tile(P, x, it, t_stage, "t_stage")
            emit_transpose_tile(P, t_stage, "t_stage", 0,
                                t_fml, "t_fml", it * 128, 1024)
        for cg in range(0, DC, 4):
            E.dma("sp",
                  P.dram["t_ag_in"][cg * 128:(cg + 4) * 128, :]
                  .rearrange("(c p) n -> p c n", p=128),
                  t_fml[:, cg * 1024:(cg + 4) * 1024]
                  .rearrange("p (c n) -> p c n", n=1024),
                  rd=[("t_fml", None)], wr=[("t_ag_in", cg)])
        E.new_block()
    E.collective("AllGather", ALU.bypass, GROUPS_B,
                 [P.dram["t_ag_in"].ap().opt()], [P.dram["t_ag"].ap().opt()],
                 rd=[("t_ag_in", None)], wr=[("t_ag", None)])
    E.new_block()


ATTN_CFG = {
    "l0a": dict(qh=16, kvh=16, dh=128, window=None, rope="i64",
                wq="l0a_wq", wk="l0a_wk", wv="l0a_wv", wo="l0a_wo",
                cos="l0a_cos", sin="l0a_sin", perm="l0a_permT"),
    "l0b": dict(qh=16, kvh=4, dh=128, window=1024, rope=None,
                wq="l0b_wq", wk="l0b_wk", wv="l0b_wv", wo="l0b_wo"),
    "l1a": dict(qh=32, kvh=8, dh=64, window=512, rope="h64",
                wq="l1a_wq", wk="l1a_wk", wv="l1a_wv", wo="l1a_wo",
                cos="l1a_cos", sin="l1a_sin", perm="l1a_permT"),
}


def emit_sub_attn(P, tag, post_hook=None):
    cfg = ATTN_CFG[tag]
    E = P.E
    x = P.sb["x"]
    emit_attn_prologue(P)

    qh_pc = cfg["qh"] // 4
    kv_pc = cfg["kvh"] // 4
    gqa = cfg["qh"] // cfg["kvh"]
    dh = cfg["dh"]
    W = cfg["window"]
    scale = 1.0 / float(np.sqrt(dh))
    NQT = S // 128
    NQB = S // 512
    t_ag = P.dram["t_ag"]
    qcols = qh_pc * dh
    kcols = kv_pc * dh
    has_rope = cfg.get("rope") is not None
    hpg = 2 if gqa == 1 else qh_pc
    ngrp = qh_pc // hpg
    kv_pg = hpg // gqa if gqa > 1 else hpg
    nqb_buf = hpg if dh == 128 else hpg // 2
    nkb_buf = kv_pg if dh == 128 else max(1, kv_pg // 2)

    def qloc(hl):
        if dh == 128:
            return hl, 0
        half = hpg // 2
        return hl % half, 64 * (hl // half)

    def kloc(kl):
        if dh == 128:
            return kl, 0
        half = max(1, kv_pg // 2)
        return kl % half, 64 * (kl // half)

    with contextlib.ExitStack() as es:
        wqh = P.sbuf("wqh", [128, DC * qcols], BF16, es)
        wkh = P.sbuf("wkh", [128, DC * kcols], BF16, es)
        wvh = P.sbuf("wvh", [128, DC * kcols], BF16, es)
        qb_a = P.sbuf("qb_a", [128, nqb_buf * S], BF16, es)
        kb_a = P.sbuf("kb_a", [128, nkb_buf * S], BF16, es)
        vb_a = P.sbuf("vb_a", [128, kv_pg * NQT * dh], BF16, es)
        nslab = 1
        tslab = P.sbuf("tslab", [128, nslab * DC * 512], BF16, es)
        qtmp = P.sbuf("qtmp", [128, 512], F32, es)
        rtmp = P.sbuf("rtmp", [128, 512], F32, es)
        RB = 3
        att = P.sbuf("att", [128, RB * 512], BF16, es)
        ev = P.sbuf("ev", [128, 512], BF16, es)
        evo = P.sbuf("evo", [128, 512], BF16, es)
        negm = P.sbuf("negm", [1, S], BF16, es)
        maxb = P.sbuf("maxb", [128, 16], F32, es)
        sums = qtmp[0:1, 0:512]
        onesrow_b = P.sbuf("onesrow_b", [1, 128], BF16, es)
        onescol_b = P.sbuf("onescol_b", [128, 1], BF16, es)
        onesrow_f = P.sbuf("onesrow_f", [1, 128], F32, es)
        identf = P.sbuf("identf", [128, 128], F32, es)
        nmask = 4 if W is None else 8
        trm5 = P.sbuf("trm5", [128, nmask * 512], BF16, es)
        if has_rope:
            rope_c = P.sbuf("rope_c", [128, 512], F32, es)
            rope_s = P.sbuf("rope_s", [128, 512], F32, es)
            permb = P.sbuf("permb", [128, 128], BF16, es)

        psA = [P.psum("psA0", [128, 512], F32, es),
               P.psum("psA1", [128, 512], F32, es)]
        psB = [P.psum("psB0", [128, 512], F32, es),
               P.psum("psB1", [128, 512], F32, es),
               P.psum("psB2", [128, 512], F32, es)]
        psB5 = psB + psA
        BNM = ["psB0n", "psB1n", "psB2n", "psA0n", "psA1n"]
        psAV = P.psum("psAV", [128, 512], F32, es)
        psSm = P.psum("psSm", [1, 512], F32, es)
        pstv = P.psum("pstv", [128, 512], BF16, es)

        E.op("gpsimd", lambda e: e.memset(onesrow_b[:], 1.0), wr=[("o1", 0)])
        E.op("gpsimd", lambda e: e.memset(onescol_b[:], 1.0), wr=[("o2", 0)])
        E.op("gpsimd", lambda e: e.memset(onesrow_f[:], 1.0), wr=[("o3", 0)])
        E.dma("sp", identf[:], P.dram["ident_in"][:, :], wr=[("identf", None)])
        E.dma("sw", trm5[:], P.dram["mask5_in"][:, 0:nmask * 512],
              wr=[("trm5", None)])
        if has_rope:
            E.dma("sw", permb[:], P.dram[cfg["perm"] + "_in"][:, :],
                  wr=[("permb", None)])
        for nm, buf, cols in ((tag + "_wq_sl", wqh, qcols),
                              (tag + "_wk_sl", wkh, kcols),
                              (tag + "_wv_sl", wvh, kcols)):
            E.dma("sw", buf[:, 0:DC * cols].rearrange("p (c n) -> p c n", n=cols),
                  P.dram[nm][:, :].rearrange("(c p) n -> p c n", p=128),
                  wr=[(nm, None)])
        E.new_block()

        def load_slab(tb8):
            sl = tb8 % nslab
            r = tb8 // 2
            col0 = (tb8 % 2) * 512
            for dc in range(DC):
                E.dma("sp", tslab[:, sl * (DC * 512) + dc * 512:
                                  sl * (DC * 512) + (dc + 1) * 512],
                      t_ag[r * D + dc * 128:r * D + (dc + 1) * 128,
                           col0:col0 + 512],
                      rd=[("t_ag", None)], wr=[("tslab", (sl, dc))])
            if has_rope:
                E.dma("sp", rope_c[:],
                      P.dram[cfg["cos"] + "_in"][:, tb8 * 512:(tb8 + 1) * 512],
                      wr=[("rope_c", 0)])
                E.dma("sp", rope_s[:],
                      P.dram[cfg["sin"] + "_in"][:, tb8 * 512:(tb8 + 1) * 512],
                      wr=[("rope_s", 0)])

        def rope_mm(psum_i, tb8, dst, dst_name, dst_slot, b0, nh):
            rsl = 0
            E.op("vector", lambda e: e.tensor_copy(ev[0:nh, :],
                                                   psA[psum_i][0:nh, :]),
                 rd=[(BNM[3 + psum_i], None)], wr=[("ev", None)])

            def f_rot(e):
                return e.matmul(psB[0][0:nh, :], permb[0:nh, 0:nh], ev[0:nh, :],
                                start=True, stop=True)

            E.op("tensor", f_rot, rd=[("permb", None), ("ev", None)],
                 wr=[(BNM[0], None)])
            E.op("vector", lambda e: e.tensor_mul(
                qtmp[0:nh, :], psA[psum_i][0:nh, :],
                rope_c[0:nh, 0:512]),
                rd=[(BNM[3 + psum_i], None), ("rope_c", rsl)], wr=[("qtmp", None)])
            E.op("vector", lambda e: e.tensor_mul(
                rtmp[0:nh, :], psB[0][0:nh, :],
                rope_s[0:nh, 0:512]),
                rd=[(BNM[0], None), ("rope_s", rsl)], wr=[("rtmp", None)])
            E.op("vector", lambda e: e.tensor_add(
                dst[b0:b0 + nh, tb8 * 512:(tb8 + 1) * 512],
                qtmp[0:nh, :], rtmp[0:nh, :]),
                rd=[("qtmp", None), ("rtmp", None)], wr=[(dst_name, dst_slot)])

        if dh == 64:
            qcur = P.sbuf("qcur", [128, S], BF16, es)
            kcur = P.sbuf("kcur", [128, S], BF16, es)

        for grp in range(ngrp):
            h0 = grp * hpg
            kv0 = h0 // gqa
            for tb8 in range(NQB):
                sl = tb8 % nslab
                load_slab(tb8)

                def projmm(e, wbuf_, cols, hcol, nh, pi, sl=sl):
                    mm = None
                    for dc in range(DC):
                        lhs = wbuf_[:, dc * cols + hcol: dc * cols + hcol + nh]
                        mm = e.matmul(psA[pi][0:nh, :], lhs,
                                      tslab[:, sl * (DC * 512) + dc * 512:
                                            sl * (DC * 512) + (dc + 1) * 512],
                                      start=(dc == 0), stop=(dc == DC - 1),
                                      skip_group_check=True)
                    return mm

                pi = 0
                for hl in range(h0, h0 + hpg):
                    bq, rq = qloc(hl - h0)
                    E.op("tensor",
                         lambda e, hl=hl, pi=pi, sl=sl:
                         projmm(e, wqh, qcols, hl * dh, dh, pi, sl),
                         rd=[(tag + "_wq_sl", None), ("tslab", (sl, None))]
                         if False else
                         [(tag + "_wq_sl", None)] + [("tslab", (sl, dc)) for dc in range(DC)],
                         wr=[(BNM[3 + pi], None)])
                    if has_rope:
                        rope_mm(pi, tb8, qb_a[:, bq * S:(bq + 1) * S], "qb_a",
                                (bq, tb8), rq, dh)
                    else:
                        E.op("vector", lambda e, bq=bq, rq=rq, tb8=tb8, pi=pi:
                             e.tensor_copy(
                                 qb_a[rq:rq + dh,
                                      bq * S + tb8 * 512:bq * S + (tb8 + 1) * 512],
                                 psA[pi][0:dh, :]),
                             rd=[(BNM[3 + pi], None)], wr=[("qb_a", (bq, tb8))])
                    pi ^= 1
                for kl in range(kv0, kv0 + kv_pg):
                    bk, rk = kloc(kl - kv0)
                    E.op("tensor",
                         lambda e, kl=kl, pi=pi, sl=sl:
                         projmm(e, wkh, kcols, kl * dh, dh, pi, sl),
                         rd=[(tag + "_wk_sl", None)] +
                         [("tslab", (sl, dc)) for dc in range(DC)],
                         wr=[(BNM[3 + pi], None)])
                    if has_rope:
                        rope_mm(pi, tb8, kb_a[:, bk * S:(bk + 1) * S], "kb_a",
                                (bk, tb8), rk, dh)
                    else:
                        E.op("vector", lambda e, bk=bk, rk=rk, tb8=tb8, pi=pi:
                             e.tensor_copy(
                                 kb_a[rk:rk + dh,
                                      bk * S + tb8 * 512:bk * S + (tb8 + 1) * 512],
                                 psA[pi][0:dh, :]),
                             rd=[(BNM[3 + pi], None)], wr=[("kb_a", (bk, tb8))])
                    pi ^= 1
                    E.op("tensor",
                         lambda e, kl=kl, pi=pi, sl=sl:
                         projmm(e, wvh, kcols, kl * dh, dh, pi, sl),
                         rd=[(tag + "_wv_sl", None)] +
                         [("tslab", (sl, dc)) for dc in range(DC)],
                         wr=[(BNM[3 + pi], None)])
                    E.op("vector", lambda e, pi=pi: e.tensor_copy(
                        ev[0:dh, :], psA[pi][0:dh, :]),
                        rd=[(BNM[3 + pi], None)], wr=[("ev", None)])

                    def f_tr(e):
                        mm = None
                        for j in range(4):
                            mm = e.matmul(
                                pstv[0:128, j * dh:(j + 1) * dh],
                                ev[0:dh, j * 128:(j + 1) * 128],
                                P.sb["ident"][0:dh, 0:dh], is_transpose=True,
                                start=(j == 0), stop=(j == 3))
                        return mm

                    E.op("tensor", f_tr, rd=[("ev", None), ("ident", None)],
                         wr=[("pstv", None)])
                    klg = kl - kv0
                    E.op("vector", lambda e, klg=klg, tb8=tb8: e.tensor_copy(
                        vb_a[:, (klg * NQT + tb8 * 4) * dh:
                             (klg * NQT + tb8 * 4 + 4) * dh],
                        pstv[0:128, 0:4 * dh]),
                        rd=[("pstv", None)], wr=[("vb_a", (klg, tb8))])
                    pi ^= 1
            E.new_block()

            last_kcopy = [None]
            for hl in range(h0, h0 + hpg):
                bq, rq = qloc(hl - h0)
                klg = (hl // gqa) - kv0
                bk, rk = kloc(klg)
                qview = qb_a[rq:rq + dh, bq * S:(bq + 1) * S]
                kview = kb_a[rk:rk + dh, bk * S:(bk + 1) * S]
                if dh == 64 and rq != 0:
                    E.dma("sp", qcur[0:dh, :], qview,
                          rd=[("qb_a", None)], wr=[("qcur", None)])
                    qview = qcur[0:dh, :]
                if dh == 64 and rk != 0:
                    if last_kcopy[0] != (bk, rk):
                        E.dma("sp", kcur[0:dh, :], kview,
                              rd=[("kb_a", None)], wr=[("kcur", None)])
                        last_kcopy[0] = (bk, rk)
                    kview = kcur[0:dh, :]

                for qt in range(NQT):
                    q0 = qt * 128
                    kb_hi = (q0 + 127) // 512
                    kb_lo = 0 if W is None else max(0, (q0 - W)) // 512
                    nkb = kb_hi - kb_lo + 1
                    for i, kb in enumerate(range(kb_lo, kb_hi + 1)):
                        pi = i % 2

                        def f_mm(e, qt=qt, kb=kb, pi=pi, qview=qview,
                                 kview=kview):
                            return e.matmul(
                                psA[pi][:, :],
                                qview[:, qt * 128:(qt + 1) * 128],
                                kview[:, kb * 512:(kb + 1) * 512],
                                start=True, stop=True, skip_group_check=True)

                        E.op("tensor", f_mm,
                             rd=[("qb_a", None), ("kb_a", None),
                                 ("qcur", None), ("kcur", None)],
                             wr=[(BNM[3 + pi], None)])
                        E.op("vector", lambda e, i=i, pi=pi: e.reduce_max(
                            maxb[:, i:i + 1], psA[pi][:], axis=AX.X),
                            rd=[(BNM[3 + pi], None)], wr=[("maxb", i)])
                    E.op("vector", lambda e, nkb=nkb: e.reduce_max(
                        maxb[:, 15:16], maxb[:, 0:nkb], axis=AX.X),
                        rd=[("maxb", None)], wr=[("maxb", 15)])

                    def f_trm(e):
                        return e.matmul(psSm[:, 0:128], maxb[:, 15:16],
                                        identf[:, 0:128],
                                        is_transpose=True, start=True, stop=True)

                    E.op("tensor", f_trm, rd=[("maxb", 15), ("identf", None)],
                         wr=[("psSm", None)])
                    E.op("vector", lambda e, qt=qt: e.tensor_scalar(
                        negm[0:1, qt * 128:(qt + 1) * 128], psSm[0:1, 0:128],
                        -1.0, None, ALU.mult),
                        rd=[("psSm", None)], wr=[("negm", qt)])

                for qb in range(NQB):
                    qb0 = qb * 512
                    kt_hi = (qb0 + 511) // 128
                    kt_lo = 0 if W is None else max(0, (qb0 - W)) // 128
                    first = True
                    for idx, kt in enumerate(range(kt_lo, kt_hi + 1)):
                        par = idx % RB
                        k0 = kt * 128

                        def f_sc(e, kt=kt, qb=qb, par=par, qview=qview,
                                 kview=kview):
                            e.matmul(psB5[par][:, :],
                                     kview[:, kt * 128:(kt + 1) * 128],
                                     qview[:, qb * 512:(qb + 1) * 512],
                                     start=True, stop=False,
                                     skip_group_check=True)
                            return e.matmul(psB5[par][:, :], onesrow_b[:],
                                            negm[0:1, qb * 512:(qb + 1) * 512],
                                            start=False, stop=True,
                                            skip_group_check=True)

                        E.op("tensor", f_sc,
                             rd=[("kb_a", None), ("qb_a", None), ("negm", None),
                                 ("qcur", None), ("kcur", None), ("o1", 0)],
                             wr=[(BNM[par], None)])
                        E.op("scalar", lambda e, par=par: e.activation(
                            att[:, par * 512:(par + 1) * 512], psB5[par][:],
                            AF.Exp, scale=scale),
                            rd=[(BNM[par], None)], wr=[("att", par)])
                        db_off = k0 - qb0
                        mi = None
                        if 0 <= db_off <= 384:
                            mi = db_off // 128
                        elif W is not None and -W <= db_off <= -W + 384:
                            mi = 4 + (db_off + W) // 128
                        if mi is not None:
                            sub = att[:, par * 512:(par + 1) * 512]
                            E.op("vector", lambda e, sub=sub, mi=mi:
                                 e.tensor_mul(sub, sub,
                                              trm5[:, mi * 512:(mi + 1) * 512]),
                                 rd=[("trm5", mi), ("att", par)],
                                 wr=[("att", par)])

                        def f_av(e, kt=kt, par=par, first=first,
                                 last=(kt == kt_hi), klg=klg, rk=rk):
                            e.matmul(psAV[0:dh, :],
                                     vb_a[:, (klg * NQT + kt) * dh:
                                          (klg * NQT + kt) * dh + dh],
                                     att[:, par * 512:(par + 1) * 512],
                                     start=first, stop=last,
                                     skip_group_check=True)
                            return e.matmul(psSm[0:1, :], onescol_b[:],
                                            att[:, par * 512:(par + 1) * 512],
                                            start=first, stop=last,
                                            skip_group_check=True)
                        _ = (first, kt)

                        E.op("tensor", f_av,
                             rd=[("vb_a", None), ("att", par), ("o2", 0)],
                             wr=[("psAV", None), ("psSm", None)])
                        first = False

                    E.op("vector", lambda e: e.tensor_copy(sums[:], psSm[0:1, :]),
                         rd=[("psSm", None)], wr=[("qtmp", None)])
                    E.op("vector", lambda e: e.reciprocal(sums[:], sums[:]),
                         rd=[("qtmp", None)], wr=[("qtmp", None)])

                    def f_bc(e):
                        return e.matmul(psB[0][:, :], onesrow_f[:], sums[:],
                                        start=True, stop=True,
                                        skip_group_check=True)

                    E.op("tensor", f_bc, rd=[("o3", 0), ("qtmp", None)],
                         wr=[(BNM[0], None)])
                    E.op("vector", lambda e: e.tensor_copy(rtmp[:], psB[0][:]),
                         rd=[(BNM[0], None)], wr=[("rtmp", None)])
                    E.op("vector", lambda e: e.tensor_mul(
                        evo[0:dh, :], psAV[0:dh, :], rtmp[0:dh, :]),
                        rd=[("psAV", None), ("rtmp", None)], wr=[("evo", None)])
                    E.dma("sp",
                          P.dram["att_pub"][hl * dh:(hl + 1) * dh,
                                            qb * 512:(qb + 1) * 512],
                          evo[0:dh, :],
                          rd=[("evo", None)], wr=[("att_pub", (hl, qb))])
        E.fence_all()
        E.new_block()

    with contextlib.ExitStack() as es:
        aout = P.sbuf("aout", [128, DC * 1024], BF16, es)
        wo_sb = P.sbuf("wo_sb", [128, DC * 2048], BF16, es)
        pso0 = P.psum("pso0", [128, 512], F32, es)
        pso1 = P.psum("pso1", [128, 512], F32, es)
        pso = [pso0, pso1]
        E.dma("sw", wo_sb[:].rearrange("p (c n) -> p c n", n=2048),
              P.dram[cfg["wo"]][:, :].rearrange("(c p) n -> p c n", p=128),
              wr=[("wo_sb", None)])
        E.collective("AllGather", ALU.bypass, GROUPS_B,
                     [P.dram["att_pub"].ap().opt()],
                     [P.dram["att_all"].ap().opt()],
                     rd=[("att_pub", None)], wr=[("att_all", None)])
        if post_hook is not None:
            post_hook(P)
        rank_sb = P.sb["rank_sb"]
        for cg in range(0, DC, 4):
            P._uniq += 1

            def f_load(e, cg=cg, _u=P._uniq):
                with e.register(f"rkoff{cg}_{_u}") as r:
                    e.reg_load(r, rank_sb[0:1, 0:1])
                    e.reg_mul(r, r, 1024)
                    e.reg_add(r, r, cg * 128 * S)
                    src_ap = bass.AP(P.dram["att_all"], r,
                                     [[S, 128], [128 * S, 4], [1, 1024]])
                    return e.dma_start(
                        out=aout[:, cg * 1024:(cg + 4) * 1024]
                        .rearrange("p (c n) -> p c n", n=1024),
                        in_=src_ap)

            E.dma_custom("sw", f_load,
                         rd=[("att_all", None), ("rank_sb", None)],
                         wr=[("aout", cg)])
        for tt in range(NT):
            for db in range(4):
                par = db % 2

                def f_mm(e, tt=tt, db=db, par=par):
                    mm = None
                    for dc in range(DC):
                        lhs = aout[:, dc * 1024 + tt * 128:
                                   dc * 1024 + (tt + 1) * 128]
                        mm = e.matmul(pso[par][:], lhs,
                                      wo_sb[:, dc * 2048 + db * 512:
                                            dc * 2048 + (db + 1) * 512],
                                      start=(dc == 0), stop=(dc == DC - 1),
                                      skip_group_check=True)
                    return mm

                E.op("tensor", f_mm,
                     rd=[("aout", None), ("wo_sb", None)], wr=[("psop", par)])

                def f_ev(e, tt=tt, db=db, par=par):
                    xs = x[:, tt * D + db * 512:tt * D + (db + 1) * 512]
                    return e.tensor_add(xs, xs, pso[par][:])

                E.op("vector", f_ev,
                     rd=[("psop", par), ("x", tt)], wr=[("x", tt)])
        E.fence_all()
        E.new_block()


def emit_sub_l1b(P, tag):
    E = P.E
    x = P.sb["x"]
    with contextlib.ExitStack() as es_outer:
        qbuf = P.sbuf("qbuf", [128, DC * 1024], BF16, es_outer)
        _emit_l1b_main(P, tag, qbuf)
        _emit_l1b_out(P, tag, qbuf)


def _emit_l1b_main(P, tag, qbuf):
    E = P.E
    x = P.sb["x"]
    with contextlib.ExitStack() as es:
        t_fml = P.sbuf("t_fml", [128, DC * 1024], BF16, es)
        with contextlib.ExitStack() as es2:
            t_stage = P.sbuf("t_stage", [128, D], BF16, es2)
            P.psum("pst", [128, 1024], BF16, es2)
            for it in range(NT):
                emit_ln_tile(P, x, it, t_stage, "t_stage")
                E.dma("sp", P.dram["t_tm_d"][it * 128:(it + 1) * 128, :],
                      t_stage[:, 0:D], rd=[("t_stage", 0)],
                      wr=[("t_tm_d", it)])
                emit_transpose_tile(P, t_stage, "t_stage", 0,
                                    t_fml, "t_fml", it * 128, 1024)
            E.fence_all()
            E.new_block()

        kvb = P.sbuf("kvb", [128, DC * 1024], BF16, es)
        kc = P.sbuf("kc", [128, 1024], F32, es)
        sqc = P.sbuf("sqc", [128, 1024], F32, es)
        wibuf = P.sbuf("wibuf", [128, 2 * DC * 128], BF16, es)
        binb = P.sbuf("binb", [128, 48], F32, es)
        ev = P.sbuf("ev", [128, 1024], BF16, es)
        qtmp = P.sbuf("qtmp", [128, 1024], F32, es)
        rtmp = P.sbuf("rtmp", [128, 1024], F32, es)
        rope_c = P.sbuf("rope_c", [128, 1024], F32, es)
        rope_s = P.sbuf("rope_s", [128, 1024], F32, es)
        permb = P.sbuf("permb", [128, 128], BF16, es)
        nrow = P.sbuf("nrow", [1, 1024], F32, es)
        wbb = P.sbuf("wbb", [128, 1024], F32, es)
        apart = P.sbuf("apart", [128, 16], F32, es)
        onescol_f = P.sbuf("onescol_f", [128, 1], F32, es)
        onesrow_f = P.sbuf("onesrow_f", [1, 128], F32, es)

        psP0 = P.psum("psP0", [128, 512], F32, es)
        psP1 = P.psum("psP1", [128, 512], F32, es)
        psS2a = P.psum("psS2a", [1, 512], F32, es)
        psS2b = P.psum("psS2b", [1, 512], F32, es)
        psS2h = [psS2a, psS2b]
        psR = P.psum("psR", [128, 512], F32, es)
        psP = [psP0, psP1]

        E.op("gpsimd", lambda e: e.memset(onescol_f[:], 1.0), wr=[("oc", 0)])
        E.op("gpsimd", lambda e: e.memset(onesrow_f[:], 1.0), wr=[("orr", 0)])
        E.dma("sw", permb[:], P.dram["l1b_perm_in"][:, :], wr=[("permb", None)])
        E.dma("sp", binb[:], P.dram["l1b_bin_in"][:, :], wr=[("binb", None)])
        E.new_block()

        def proj_chunk(c, which, bias_col):
            outc = which * DC + c
            wslot = outc % 2
            E.dma("sw", wibuf[:, wslot * (DC * 128):(wslot + 1) * (DC * 128)]
                  .rearrange("p (c n) -> p c n", n=128),
                  P.dram["l1b_win"][:, outc * 128:(outc + 1) * 128]
                  .rearrange("(c p) n -> p c n", p=128),
                  wr=[("wibuf", wslot)])
            for blk in range(2):
                def f_mm(e, wslot=wslot, blk=blk):
                    mm = None
                    for dc in range(DC):
                        lhs = wibuf[:, wslot * (DC * 128) + dc * 128:
                                    wslot * (DC * 128) + (dc + 1) * 128]
                        mm = e.matmul(psP[blk][:],
                                      lhs,
                                      t_fml[:, dc * 1024 + blk * 512:
                                            dc * 1024 + blk * 512 + 512],
                                      start=(dc == 0), stop=(dc == DC - 1),
                                      skip_group_check=True)
                    return mm

                E.op("tensor", f_mm, rd=[("wibuf", wslot), ("t_fml", None)],
                     wr=[("psP", blk)])
            for blk in range(2):
                E.op("vector", lambda e, blk=blk, bias_col=bias_col:
                     e.tensor_scalar(qtmp[:, blk * 512:(blk + 1) * 512],
                                     psP[blk][:], binb[:, bias_col:bias_col + 1],
                                     None, ALU.add),
                     rd=[("psP", blk), ("binb", None)], wr=[("qtmp", blk)])

        def rope_l1b(c, dst, dst_name, dst_f32):
            E.dma("sp", rope_c[:], P.dram["l1b_cos_sl"]
                  [c * 128:(c + 1) * 128, :], wr=[("rope_c", None)])
            E.dma("sp", rope_s[:], P.dram["l1b_sin_sl"]
                  [c * 128:(c + 1) * 128, :], wr=[("rope_s", None)])
            E.op("vector", lambda e: e.tensor_copy(ev[:], qtmp[:]),
                 rd=[("qtmp", None)], wr=[("ev", None)])
            for blk in range(2):
                def f_rot(e, blk=blk):
                    return e.matmul(psR[:], permb[:],
                                    ev[:, blk * 512:(blk + 1) * 512],
                                    start=True, stop=True)

                E.op("tensor", f_rot, rd=[("permb", None), ("ev", None)],
                     wr=[("psR", 0)])
                E.op("vector", lambda e, blk=blk: e.tensor_mul(
                    rtmp[:, blk * 512:(blk + 1) * 512], psR[:],
                    rope_s[:, blk * 512:(blk + 1) * 512]),
                    rd=[("psR", 0), ("rope_s", None)], wr=[("rtmp", blk)])
            E.op("vector", lambda e: e.tensor_mul(qtmp[:], qtmp[:], rope_c[:]),
                 rd=[("qtmp", None), ("rope_c", None)], wr=[("qtmp", None)])
            if dst_f32:
                E.op("vector", lambda e: e.tensor_add(dst, qtmp[:], rtmp[:]),
                     rd=[("qtmp", None), ("rtmp", None)], wr=[(dst_name, c)])
            else:
                E.op("vector", lambda e: e.tensor_add(dst, qtmp[:], rtmp[:]),
                     rd=[("qtmp", None), ("rtmp", None)], wr=[(dst_name, c)])

        for c in range(DC):
            proj_chunk(c, 0, c)
            if c < 8:
                rope_l1b(c, qbuf[:, c * 1024:(c + 1) * 1024], "qbuf", False)
            else:
                E.op("vector", lambda e, c=c: e.tensor_copy(
                    qbuf[:, c * 1024:(c + 1) * 1024], qtmp[:]),
                    rd=[("qtmp", None)], wr=[("qbuf", c)])
            proj_chunk(c, 1, DC + c)
            if c < 8:
                rope_l1b(c, kc[:], "kc", True)
            else:
                E.op("vector", lambda e: e.tensor_copy(kc[:], qtmp[:]),
                     rd=[("qtmp", None)], wr=[("kc", None)])
            E.op("scalar", lambda e: e.activation(sqc[:], kc[:], AF.Square),
                 rd=[("kc", None)], wr=[("sqc", None)])
            for blk in range(2):
                def f_s2(e, c=c, blk=blk):
                    return e.matmul(psS2h[blk][0:1, :],
                                    onescol_f[:], sqc[:, blk * 512:(blk + 1) * 512],
                                    start=(c == 0),
                                    stop=(c == DC - 1), skip_group_check=True)

                E.op("tensor", f_s2, rd=[("oc", 0), ("sqc", None)],
                     wr=[("psS2", blk)])
            proj_chunk(c, 2, 2 * DC + c)
            E.op("vector", lambda e, c=c: e.tensor_mul(
                kvb[:, c * 1024:(c + 1) * 1024], qtmp[:], kc[:]),
                rd=[("qtmp", None), ("kc", None)], wr=[("kvb", c)])
        E.new_block()

        for blk in range(2):
            E.op("scalar", lambda e, blk=blk: e.activation(
                nrow[0:1, blk * 512:(blk + 1) * 512], psS2h[blk][0:1, :],
                AF.Sqrt),
                rd=[("psS2", blk)], wr=[("nrow", blk)])
        E.op("vector", lambda e: e.tensor_scalar(nrow[:], nrow[:], 1e-12, None,
                                                 ALU.max),
             rd=[("nrow", None)], wr=[("nrow", None)])
        E.op("vector", lambda e: e.reciprocal(nrow[:], nrow[:]),
             rd=[("nrow", None)], wr=[("nrow", None)])
        for blk in range(2):
            def f_bc(e, blk=blk):
                return e.matmul(psR[:], onesrow_f[:],
                                nrow[0:1, blk * 512:(blk + 1) * 512],
                                start=True, stop=True)

            E.op("tensor", f_bc, rd=[("orr", 0), ("nrow", None)],
                 wr=[("psR", 0)])
            E.op("vector", lambda e, blk=blk: e.tensor_copy(
                wbb[:, blk * 512:(blk + 1) * 512], psR[:]),
                rd=[("psR", 0)], wr=[("wbb", blk)])
        for c in range(DC):
            E.op("vector", lambda e, c=c: e.tensor_mul(
                rtmp[:], kvb[:, c * 1024:(c + 1) * 1024], wbb[:]),
                rd=[("kvb", c), ("wbb", None)], wr=[("rtmp", None)])
            E.op("vector", lambda e, c=c: e.reduce_sum(
                apart[:, c:c + 1], rtmp[:], axis=AX.X),
                rd=[("rtmp", None)], wr=[("apart", c)])
        E.dma("sp", P.dram["ar_in"][:, :], apart[:],
              rd=[("apart", None)], wr=[("ar_in", None)])
        E.new_block()
        E.collective("AllReduce", ALU.add, GROUPS_B,
                     [P.dram["ar_in"].ap().opt()], [P.dram["ar_out"].ap().opt()],
                     rd=[("ar_in", None)], wr=[("ar_out", None)])
        E.dma("sp", apart[:], P.dram["ar_out"][:, :],
              rd=[("ar_out", None)], wr=[("apart", None)])
        E.new_block()
        for c in range(DC):
            E.op("vector", lambda e, c=c: e.tensor_scalar(
                qbuf[:, c * 1024:(c + 1) * 1024],
                qbuf[:, c * 1024:(c + 1) * 1024],
                apart[:, c:c + 1], None, ALU.mult),
                rd=[("qbuf", c), ("apart", None)], wr=[("qbuf", c)])
        E.fence_all()
        E.new_block()


def _emit_l1b_out(P, tag, qbuf):
    E = P.E
    x = P.sb["x"]
    with contextlib.ExitStack() as es:
        wo_sb = P.sbuf("wo_sb", [128, DC * 2048], BF16, es)
        bb = P.sbuf("bb", [128, 2048], F32, es)
        brow = P.sbuf("brow", [1, 2048], F32, es)
        onesrow_f2 = P.sbuf("onesrow_f2", [1, 128], F32, es)
        ttile = P.sbuf("ttile", [128, 2 * D], BF16, es)
        rtmp2 = P.sbuf("rtmp2", [128, 512], F32, es)
        pso0 = P.psum("pso0", [128, 512], F32, es)
        pso1 = P.psum("pso1", [128, 512], F32, es)
        psR2 = P.psum("psR2", [128, 512], F32, es)
        pso = [pso0, pso1]

        E.op("gpsimd", lambda e: e.memset(onesrow_f2[:], 1.0), wr=[("or2", 0)])
        E.dma("sp", brow[0:1, :], P.dram["l1b_bout_in"][:, :],
              wr=[("brow", None)])
        E.dma("sw", wo_sb[:].rearrange("p (c n) -> p c n", n=2048),
              P.dram["l1b_wout"][:, :].rearrange("(c p) n -> p c n", p=128),
              wr=[("wo_sb", None)])
        for db in range(4):
            def f_bc(e, db=db):
                return e.matmul(psR2[:], onesrow_f2[:],
                                brow[0:1, db * 512:(db + 1) * 512],
                                start=True, stop=True)

            E.op("tensor", f_bc, rd=[("or2", 0), ("brow", None)],
                 wr=[("psR2", 0)])
            E.op("vector", lambda e, db=db: e.tensor_copy(
                bb[:, db * 512:(db + 1) * 512], psR2[:]),
                rd=[("psR2", 0)], wr=[("bb", db)])
        E.new_block()
        for tt in range(NT):
            tslot = tt % 2
            E.dma("sp", ttile[:, tslot * D:(tslot + 1) * D],
                  P.dram["t_tm_d"][tt * 128:(tt + 1) * 128, :],
                  rd=[("t_tm_d", tt)], wr=[("ttile", tslot)])
            for db in range(4):
                par = db % 2

                def f_mm(e, tt=tt, db=db, par=par):
                    mm = None
                    for dc in range(DC):
                        lhs = qbuf[:, dc * 1024 + tt * 128:
                                   dc * 1024 + (tt + 1) * 128]
                        mm = e.matmul(pso[par][:],
                                      lhs,
                                      wo_sb[:, dc * 2048 + db * 512:
                                            dc * 2048 + (db + 1) * 512],
                                      start=(dc == 0), stop=(dc == DC - 1),
                                      skip_group_check=True)
                    return mm

                E.op("tensor", f_mm,
                     rd=[("qbuf", None), ("wo_sb", None)], wr=[("psop", par)])

                def f_e1(e, tt=tt, db=db, par=par, tslot=tslot):
                    return e.tensor_add(rtmp2[:], pso[par][:],
                                        bb[:, db * 512:(db + 1) * 512])

                E.op("vector", f_e1, rd=[("psop", par), ("bb", db)],
                     wr=[("rtmp2", None)])

                def f_e2(e, tt=tt, db=db, tslot=tslot):
                    return e.tensor_add(
                        rtmp2[:], rtmp2[:],
                        ttile[:, tslot * D + db * 512:tslot * D + (db + 1) * 512])

                E.op("vector", f_e2, rd=[("rtmp2", None), ("ttile", tslot)],
                     wr=[("rtmp2", None)])

                def f_e3(e, tt=tt, db=db):
                    xs = x[:, tt * D + db * 512:tt * D + (db + 1) * 512]
                    return e.tensor_add(xs, xs, rtmp2[:])

                E.op("vector", f_e3, rd=[("rtmp2", None), ("x", tt)],
                     wr=[("x", tt)])
        E.fence_all()
        E.new_block()


def emit_expand_l0(P):
    E = P.E
    nc = P.nc
    ident = P.sb["ident"]
    with contextlib.ExitStack() as es:
        lut_sb = P.sbuf("lut_sb", [1, 16], F32, es)
        onesr = P.sbuf("onesr", [1, 128], F32, es)
        lutb = P.sbuf("lutb", [128, 16], F32, es)
        lutI = P.sbuf("lutI", [128, 16 * 128], BF16, es)
        idxb = P.sbuf("idxb", [128, 5504], BF16, es)
        msk = P.sbuf("msk", [128, 2 * 512], BF16, es)
        wrow = P.sbuf("wrow", [128, 5504], BF16, es)
        psl = P.psum("ps_lut", [128, 16], F32, es)
        psw = P.psum("ps_w", [128, 512], F32, es)

        E.dma("sp", lut_sb[0:1, :], P.dram["lut_in"][:, :], wr=[("lut_sb", None)])
        E.op("gpsimd", lambda e: e.memset(onesr[:], 1.0), wr=[("onesr", None)])
        E.op("tensor", lambda e: e.matmul(psl[:], onesr[0:1, :], lut_sb[0:1, :],
                                          start=True, stop=True),
             rd=[("onesr", None), ("lut_sb", None)], wr=[("ps_lut", None)])
        E.op("vector", lambda e: e.tensor_copy(lutb[:], psl[:]),
             rd=[("ps_lut", None)], wr=[("lutb", None)])
        for v in range(16):
            E.op("vector", lambda e, v=v: e.tensor_scalar(
                lutI[:, v * 128:(v + 1) * 128], ident[:], lutb[:, v:v + 1],
                None, ALU.mult),
                rd=[("ident", None), ("lutb", None)], wr=[("lutI", v)])

        def expand(idx_dram, out_dram, rows, cols, rows_done_name):
            nch = (rows + 127) // 128
            for ch in range(nch):
                r0 = ch * 128
                nr = min(128, rows - r0)
                E.dma("sw", idxb[0:nr, 0:cols], idx_dram[r0:r0 + nr, :],
                      wr=[("idxb", None)])
                nblk = (cols + 511) // 512
                for ob in range(nblk):
                    c0 = ob * 512
                    nc_ = min(512, cols - c0)
                    for v in range(16):
                        E.op("vector", lambda e, v=v, nr=nr, c0=c0, nc_=nc_:
                             e.tensor_scalar(msk[0:nr, (v % 2) * 512:(v % 2) * 512 + nc_],
                                             idxb[0:nr, c0:c0 + nc_],
                                             float(v), None, ALU.is_equal),
                             rd=[("idxb", None)], wr=[("msk", v % 2)])
                        E.op("tensor", lambda e, v=v, nr=nr, nc_=nc_:
                             e.matmul(psw[0:nr, 0:nc_],
                                      lutI[0:nr, v * 128:v * 128 + nr],
                                      msk[0:nr, (v % 2) * 512:(v % 2) * 512 + nc_],
                                      start=(v == 0), stop=(v == 15)),
                             rd=[("lutI", v), ("msk", v % 2)], wr=[("ps_w", None)])
                    E.op("vector", lambda e, nr=nr, c0=c0, nc_=nc_:
                         e.tensor_copy(wrow[0:nr, c0:c0 + nc_], psw[0:nr, 0:nc_]),
                         rd=[("ps_w", None)], wr=[("wrow", ob)])
                E.dma("sp", out_dram[r0:r0 + nr, :], wrow[0:nr, 0:cols],
                      rd=[("wrow", None)], wr=[(rows_done_name, ch)])

        expand(P.dram["gidx_sl"], P.dram["wexp_g_in"], 256, 5504, "wexp_g_in")
        expand(P.dram["uidx_sl"], P.dram["wexp_u_in"], 256, 5504, "wexp_u_in")
        expand(P.dram["didx_sl"], P.dram["wexp_d_in"], 688, 2048, "wexp_d_in")
        for nm, out_nm in (("wexp_g_in", "wexp_g"), ("wexp_u_in", "wexp_u"),
                           ("wexp_d_in", "wexp_d")):
            E.collective("AllGather", ALU.bypass, GROUP_ALL,
                         [P.dram[nm].ap().opt()], [P.dram[out_nm].ap().opt()],
                         rd=[(nm, None)], wr=[(out_nm, None)])


def emit_sub_l0f(P, tag):
    E = P.E
    wg = P.dram["wexp_g"]
    wu = P.dram["wexp_u"]
    wd = P.dram["wexp_d"]
    for tb in range(NT // 4):
        with contextlib.ExitStack() as es_tb:
            hfm = P.sbuf("hfm", [128, 43 * 512], BF16, es_tb)
            _l0f_block(P, tag, wg, wu, wd, tb, hfm)


def _l0f_block(P, tag, wg, wu, wd, tb, hfm):
    E = P.E
    x = P.sb["x"]
    IC = 43
    ident = P.sb["ident"]
    es = contextlib.ExitStack()
    with es:
        t_stage = P.sbuf("t_stage", [128, D], BF16, es)
        t_fm4 = P.sbuf("t_fm4", [128, DC * 512], BF16, es)
        wbuf = P.sbuf("wbuf", [128, 2 * DC * 512], BF16, es)
        hb = P.sbuf("hb", [128, 2 * 512], BF16, es)
        silu_g = P.sbuf("silu_g", [128, 2 * 512], BF16, es)
        psg = [P.psum("ps_a0", [128, 512], F32, es),
               P.psum("ps_a1", [128, 512], F32, es)]
        psu = [P.psum("ps_b0", [128, 512], F32, es),
               P.psum("ps_b1", [128, 512], F32, es)]
        P.psum("pst", [128, 1024], BF16, es)

        for itl in range(4):
            it = tb * 4 + itl
            emit_ln_tile(P, x, it, t_stage, "t_stage")
            emit_transpose_tile(P, t_stage, "t_stage", 0,
                                t_fm4, "t_fm4", itl * 128, 512)
        E.new_block()

        ncols = 512
        NBLK = (5504 + 511) // 512
        pending = []

        def emit_tile_transpose(jb, itl, nch, pp):
            def f_tr(e, nch=nch, pp=pp):
                mm = None
                for j in range(nch):
                    mm = e.matmul(P.ps["pst"][:, j * 128:(j + 1) * 128],
                                  hb[:, pp * 512 + j * 128:
                                      pp * 512 + (j + 1) * 128],
                                  ident[:], is_transpose=True,
                                  start=(j == 0), stop=(j == nch - 1))
                return mm

            E.op("tensor", f_tr, rd=[("hb", pp), ("ident", None)],
                 wr=[("pst", 0)])

            def f_ev(e, itl=itl, jb=jb, nch=nch):
                ins = None
                for j in range(nch):
                    c = 4 * jb + j
                    ins = e.tensor_copy(
                        hfm[:, c * 512 + itl * 128: c * 512 + (itl + 1) * 128],
                        P.ps["pst"][:, j * 128:(j + 1) * 128])
                return ins

            E.op("vector", f_ev, rd=[("pst", 0)],
                 wr=[("hfm", (itl, jb))])

        for jb in range(NBLK):
            c0 = jb * 512
            bw = min(512, 5504 - c0)
            for which, w in ((0, wg), (1, wu)):
                for hh in range(2):
                    base = which * (DC * 512) + hh * (DC // 2) * bw
                    dst = wbuf[:, base: base + (DC // 2) * bw]
                    src = w[hh * 1024:(hh + 1) * 1024, c0:c0 + bw]
                    E.dma("sp" if which == 0 else "act",
                          dst.rearrange("p (c n) -> p c n", n=bw),
                          src.rearrange("(c p) n -> p c n", p=128),
                          rd=[], wr=[("wbuf", which)])
            for itl in range(4):
                it = tb * 4 + itl
                pp = itl % 2

                def f_gate(e, itl=itl, bw=bw, pp=pp):
                    mm = None
                    for c in range(DC):
                        lhs = t_fm4[:, c * ncols + itl * 128:
                                    c * ncols + (itl + 1) * 128]
                        mm = e.matmul(psg[pp][:, 0:bw], lhs,
                                      wbuf[:, c * bw:(c + 1) * bw],
                                      start=(c == 0), stop=(c == DC - 1))
                    return mm

                def f_up(e, itl=itl, bw=bw, pp=pp):
                    mm = None
                    for c in range(DC):
                        lhs = t_fm4[:, c * ncols + itl * 128:
                                    c * ncols + (itl + 1) * 128]
                        mm = e.matmul(psu[pp][:, 0:bw], lhs,
                                      wbuf[:, DC * 512 + c * bw:
                                           DC * 512 + (c + 1) * bw],
                                      start=(c == 0), stop=(c == DC - 1))
                    return mm

                E.op("tensor", f_up,
                     rd=[("t_fm4", None), ("wbuf", 1)], wr=[("ps_b", pp)])
                E.op("tensor", f_gate,
                     rd=[("t_fm4", None), ("wbuf", 0)], wr=[("ps_a", pp)])
                E.op("scalar", lambda e, bw=bw, pp=pp: e.activation(
                    silu_g[:, pp * 512:pp * 512 + bw], psg[pp][:, 0:bw], AF.Silu),
                    rd=[("ps_a", pp)], wr=[("silu", pp)])
                E.op("vector", lambda e, bw=bw, pp=pp: e.tensor_mul(
                    hb[:, pp * 512:pp * 512 + bw],
                    silu_g[:, pp * 512:pp * 512 + bw], psu[pp][:, 0:bw]),
                    rd=[("silu", pp), ("ps_b", pp)], wr=[("hb", pp)])
                nch = bw // 128
                for fn in pending:
                    fn()
                pending = [lambda jb=jb, itl=itl, nch=nch, pp=pp:
                           emit_tile_transpose(jb, itl, nch, pp)]
        for fn in pending:
            fn()
        E.new_block()
    es2 = contextlib.ExitStack()
    with es2:
        P.sbuf("wdbuf", [128, 2 * 1024], BF16, es2)
        for tt in range(4):
            for db in range(2):
                P.psum(f"ps_dn{tt}_{db}", [128, 512], F32, es2)
        emit_ffn_down(P, x, hfm, wd, tb * 4, IC)
        E.new_block()
        E.fence_all()
        E.new_block()


def build_program(subs):
    P = Prog()
    E = P.E

    x_in = P.dram_in("x_in", [TPC, D], F32)
    x_out = P.dram_out("x_out", [TPC, D], F32)
    P.dram_in("ident_in", [128, 128], F32)
    need = set(subs)

    if need & {"l1f1", "l1f2"}:
        P.dram_in("l1f_wg", [4096, 4096], F32)
        P.dram_in("l1f_wu", [4096, 4096], F32)
        P.dram_in("l1f_wd", [8192, D], F32)
        P.dram_tmp("l1f_wg_bf", [4096, 4096], BF16)
        P.dram_tmp("l1f_wu_bf", [4096, 4096], BF16)
        P.dram_tmp("l1f_wd_bf", [8192, D], BF16)
    attn_subs = [s for s in subs if s in ATTN_CFG]
    if attn_subs:
        if DBG_ATTN:
            P.dram_out("dbg_v2", [128, S], F32)
            P.dram_out("dbg_q", [128, S], F32)
            P.dram_out("dbg_k", [128, S], F32)
            P.dram_out("dbg_v", [128, S], F32)
        P.dram_in("mask5_in", [128, 8 * 512], F32)
        P.dram_tmp("t_ag_in", [D, TPC], BF16)
        P.dram_tmp("t_ag", [4 * D, TPC], BF16)
        P.dram_tmp("att_pub", [512, S], BF16)
        P.dram_tmp("att_all", [D, S], BF16)
        P.dram_in("rank_in", [1, 1], I32)
        for tg in set(attn_subs):
            cfg = ATTN_CFG[tg]
            qh_pc = cfg["qh"] // 4
            kv_pc = cfg["kvh"] // 4
            P.dram_in(tg + "_wq_sl", [D, qh_pc * cfg["dh"]], F32)
            P.dram_in(tg + "_wk_sl", [D, kv_pc * cfg["dh"]], F32)
            P.dram_in(tg + "_wv_sl", [D, kv_pc * cfg["dh"]], F32)
            P.dram_in(cfg["wo"], [D, D], F32)
            if cfg.get("rope"):
                P.dram_in(cfg["cos"] + "_in", [128, S], F32)
                P.dram_in(cfg["sin"] + "_in", [128, S], F32)
                P.dram_in(cfg["perm"] + "_in", [128, 128], F32)
    if "l1b" in need:
        P.dram_in("l1b_win", [D, 3 * D], F32)
        P.dram_in("l1b_bin_in", [128, 48], F32)
        P.dram_in("l1b_wout", [D, D], F32)
        P.dram_in("l1b_bout_in", [1, D], F32)
        P.dram_in("l1b_cos_sl", [1024, TPC], F32)
        P.dram_in("l1b_sin_sl", [1024, TPC], F32)
        P.dram_in("l1b_perm_in", [128, 128], F32)
        P.dram_tmp("t_tm_d", [TPC, D], BF16)
        P.dram_tmp("ar_in", [128, 16], F32)
        P.dram_tmp("ar_out", [128, 16], F32)
    if need & {"l0f1", "l0f2"}:
        P.dram_in("lut_in", [1, 16], F32)
        P.dram_in("gidx_sl", [256, 5504], I32)
        P.dram_in("uidx_sl", [256, 5504], I32)
        P.dram_in("didx_sl", [688, 2048], I32)
        P.dram_tmp("wexp_g_in", [256, 5504], BF16)
        P.dram_tmp("wexp_u_in", [256, 5504], BF16)
        P.dram_tmp("wexp_d_in", [688, 2048], BF16)
        P.dram_tmp("wexp_g", [2048, 5504], BF16)
        P.dram_tmp("wexp_u", [2048, 5504], BF16)
        P.dram_tmp("wexp_d", [5504, 2048], BF16)

    with P.es:
        x = P.sbuf("x", [128, NT * D], F32)
        P.sbuf("stats", [128, 8], F32)
        ident = P.sbuf("ident", [128, 128], BF16)
        if attn_subs:
            rank_sb = P.sbuf("rank_sb", [1, 1], I32)
            E.dma("sp", rank_sb[0:1, :], P.dram["rank_in"][:, :],
                  wr=[("rank_sb", None)])

        E.dma("sp", x[:].rearrange("p (n d) -> p n d", d=D),
              x_in[:, :].rearrange("(n p) d -> p n d", p=128),
              wr=[("x", None)])
        E.dma("sw", ident[:], P.dram["ident_in"][:, :], wr=[("ident", None)])

        expand_pending = bool(need & {"l0f1", "l0f2"})
        if expand_pending and "l0a" not in subs:
            emit_expand_l0(P)
            expand_pending = False

        def emit_cast(nm):
            src = P.dram[nm]
            dst = P.dram[nm + "_bf"]
            rows = src.shape[0]
            step = 256
            for r0 in range(0, rows, step):
                E.dma("sw", dst[r0:r0 + step, :], src[r0:r0 + step, :],
                      wr=[(nm + "_bf", r0 // step)])

        cast_after = {}
        if need & {"l1f1", "l1f2"}:
            seq = list(subs)
            def place(nm, after_idx):
                key = seq[min(after_idx, len(seq) - 1)]
                cast_after.setdefault(key, []).append(nm)
            place("l1f_wg", 0)
            place("l1f_wu", 1)
            place("l1f_wd", 3)
        E.new_block()

        for sub in subs:
            E.rotate_qsems(sub)
            if sub in ("l1f1", "l1f2"):
                E.fence_all("gpsimd")
                E.new_block()
                emit_sub_l1f(P, sub, P.dram["l1f_wg_bf"], P.dram["l1f_wu_bf"],
                             P.dram["l1f_wd_bf"])
            elif sub in ("l0f1", "l0f2"):
                emit_sub_l0f(P, sub)
            elif sub in ATTN_CFG:
                hook = None
                if sub == "l0a" and expand_pending:
                    hook = emit_expand_l0
                    expand_pending = False
                emit_sub_attn(P, sub, post_hook=hook)
            elif sub == "l1b":
                emit_sub_l1b(P, sub)
            else:
                raise NotImplementedError(sub)
            for nm in cast_after.get(sub, ()):
                emit_cast(nm)

        E.dma("sp", x_out[:, :].rearrange("(n p) d -> p n d", p=128),
              x[:].rearrange("p (n d) -> p n d", d=D), rd=[("x", None)])
        E.fence_all("sync")
        E.realize()
    return P


def _shard_x(xfull):
    xf = np.ascontiguousarray(np.asarray(xfull).reshape(B * S, D))
    return [np.ascontiguousarray(xf[c * TPC:(c + 1) * TPC]) for c in range(NCORES)]


def make_in_maps(inputs, subs, x_cur, w=None, c=None):
    w = w if w is not None else host_prepare(inputs)
    c = c if c is not None else host_consts()
    shards = _shard_x(x_cur)
    need = set(subs)
    in_maps = []
    for core in range(NCORES):
        m = {"x_in": shards[core], "ident_in": c["ident"]}
        if need & {"l1f1", "l1f2"}:
            m["l1f_wg"] = w["l1f_wg"]
            m["l1f_wu"] = w["l1f_wu"]
            m["l1f_wd"] = w["l1f_wd"]
        for tg in need & set(ATTN_CFG):
            cfg = ATTN_CFG[tg]
            qh_pc = cfg["qh"] // 4
            kv_pc = cfg["kvh"] // 4
            g = core % 4
            m["mask5_in"] = c["mask5"]
            m["rank_in"] = np.array([[core % 4]], np.int32)
            m[tg + "_wq_sl"] = np.ascontiguousarray(
                w[cfg["wq"]][:, g * qh_pc * cfg["dh"]:(g + 1) * qh_pc * cfg["dh"]])
            m[tg + "_wk_sl"] = np.ascontiguousarray(
                w[cfg["wk"]][:, g * kv_pc * cfg["dh"]:(g + 1) * kv_pc * cfg["dh"]])
            m[tg + "_wv_sl"] = np.ascontiguousarray(
                w[cfg["wv"]][:, g * kv_pc * cfg["dh"]:(g + 1) * kv_pc * cfg["dh"]])
            m[cfg["wo"]] = w[cfg["wo"]]
            if cfg.get("rope"):
                m[cfg["cos"] + "_in"] = c[cfg["cos"]]
                m[cfg["sin"] + "_in"] = c[cfg["sin"]]
                m[cfg["perm"] + "_in"] = c[cfg["perm"]]
        if "l1b" in need:
            tok0 = (core % 4) * TPC
            m["l1b_win"] = w["l1b_win"]
            m["l1b_bin_in"] = w["l1b_bin"]
            m["l1b_wout"] = w["l1b_wout"]
            m["l1b_bout_in"] = w["l1b_bout"]
            m["l1b_cos_sl"] = np.ascontiguousarray(c["l1b_cosT"][:, tok0:tok0 + TPC])
            m["l1b_sin_sl"] = np.ascontiguousarray(c["l1b_sinT"][:, tok0:tok0 + TPC])
            m["l1b_perm_in"] = c["l1b_permT"]
        if need & {"l0f1", "l0f2"}:
            m["lut_in"] = w["l0f_lut"]
            m["gidx_sl"] = np.ascontiguousarray(
                w["l0f_gidxT"][core * 256:(core + 1) * 256])
            m["uidx_sl"] = np.ascontiguousarray(
                w["l0f_uidxT"][core * 256:(core + 1) * 256])
            m["didx_sl"] = np.ascontiguousarray(
                w["l0f_didxT"][core * 688:(core + 1) * 688])
        in_maps.append(m)
    return in_maps


def run_subset(inputs, subs, x_cur=None, w=None, c=None):
    if x_cur is None:
        x_cur = inputs["L0_attn_a_hidden_states"]
    P = build_program(subs)
    in_maps = make_in_maps(inputs, subs, x_cur, w=w, c=c)
    res = run_bass_kernel_spmd(P.nc, in_maps, core_ids=list(range(NCORES)))
    out = np.concatenate([res.results[cc]["x_out"] for cc in range(NCORES)], axis=0)
    return np.ascontiguousarray(out.reshape(B, S, D)).astype(np.float32)


ALL_SUBS = ["l0a", "l0f1", "l0b", "l0f2", "l1a", "l1f1", "l1b", "l1f2"]


def kernel(**inputs):
    return run_subset(inputs, ALL_SUBS)
